# version 25
# baseline (speedup 1.0000x reference)
import numpy as np
from contextlib import ExitStack

import concourse.bass as bass
import concourse.mybir as mybir
from concourse.bass_utils import run_bass_kernel_spmd

F32 = mybir.dt.float32
BF16 = mybir.dt.bfloat16
I32 = mybir.dt.int32
AF = mybir.ActivationFunctionType
ALU = mybir.AluOpType
AX = mybir.AxisListType

B, L, E, H, G, W, D = 2, 16384, 1024, 16, 2, 256, 64
HID = 3072
NCORE = 8
QPC = 4096
NW = 16
AW = 2176
ACH = AW // 128
PCH = QPC // 128
NCH = PCH + ACH
TGRP = 512
NGRP = QPC // TGRP
CPG = TGRP // 128
U_SHIFT = 12.0
EPS = 1e-5
INV127 = 1.0 / 127.0
INV127SQ = 1.0 / (127.0 * 127.0)
TT_SPLITS = [(0, 512), (512, 512), (1024, 512), (1536, 512), (2048, 128)]


class Sem:
    def __init__(self, h):
        self.h = h
        self.n = 0

    def inc(self, k=1):
        self.n += k
        return self.n


class Em:
    ENGS = ("sync", "gpsimd", "tensor", "vector", "scalar")

    def __init__(self):
        self.q = {e: [] for e in self.ENGS}

    def add(self, eng, fn):
        self.q[eng].append(fn)

    def wait(self, eng, tok):
        if tok is not None:
            self.add(eng, lambda e, s=tok[0], v=tok[1]: e.wait_ge(s, v))

    def run(self, blk):
        for eng in self.ENGS:
            fns = self.q[eng]
            getattr(blk, eng)(lambda e, fns=fns: [f(e) for f in fns])


def _rope_tables():
    dpos = np.arange(0, D, 2, dtype=np.float64)
    inv = np.exp(dpos * (-np.log(10000.0) / D))
    pa = np.zeros((64, 64), np.float32)
    pb = np.zeros((64, 64), np.float32)
    for dp in range(32):
        pa[dp, 2 * dp] = 1.0
        pa[32 + dp, 2 * dp + 1] = 1.0
        pb[dp, 2 * dp + 1] = -1.0
        pb[32 + dp, 2 * dp] = 1.0
    z = np.zeros_like(pa)
    pa128 = np.block([[pa, z], [z, pa]])
    pb128 = np.block([[pb, z], [z, pb]])
    paT = np.ascontiguousarray(pa128.T)
    pbT = np.ascontiguousarray(pb128.T)
    freqs128 = np.concatenate([inv, inv, inv, inv])
    return paT, pbT, freqs128


_PAT, _PBT, _FREQS128 = _rope_tables()


def _core_tables(qq):
    a0 = 2048 * qq
    pos = (a0 + np.arange(AW, dtype=np.float64))[None, :]
    ang = _FREQS128[:, None] * pos
    return np.cos(ang).astype(np.float32), np.sin(ang).astype(np.float32)


def build_nc(stages=3):
    nc = bass.Bass("TRN2", target_bir_lowering=False)
    ap = lambda name, shape, dt, kind: nc.dram_tensor(name, shape, dt, kind=kind).ap()

    x_attn = ap("x_attn", [AW, E], F32, "ExternalInput")
    x_ffn = ap("x_ffn", [QPC, E], F32, "ExternalInput")
    wq1 = ap("wq1", [E, E], F32, "ExternalInput")
    wq2 = ap("wq2", [E, E], F32, "ExternalInput")
    wk1 = ap("wk1", [E, 128], F32, "ExternalInput")
    wk2 = ap("wk2", [E, 128], F32, "ExternalInput")
    wv = ap("wv", [E, 128], F32, "ExternalInput")
    wout = ap("wout", [E, E], F32, "ExternalInput")
    bout = ap("bout", [1, E], F32, "ExternalInput")
    lvec = ap("lvec", [1, 256], F32, "ExternalInput")
    wg_i = ap("wg_i", [HID, E], I32, "ExternalInput")
    wu_i = ap("wu_i", [HID, E], I32, "ExternalInput")
    wd_i = ap("wd_i", [E, HID], I32, "ExternalInput")
    tcos = ap("tcos", [128, AW], F32, "ExternalInput")
    tsin = ap("tsin", [128, AW], F32, "ExternalInput")
    paT = ap("paT", [128, 128], F32, "ExternalInput")
    pbT = ap("pbT", [128, 128], F32, "ExternalInput")
    iden = ap("iden", [128, 128], F32, "ExternalInput")

    out = ap("out", [QPC, E], F32, "ExternalOutput")

    scr_gu = ap("scr_gu", [24 * 128, 2048], BF16, "Internal")
    scr_g = ap("scr_g", [QPC, E], F32, "Internal")

    with ExitStack() as ctx:
        sb = lambda n, s, d: ctx.enter_context(nc.sbuf_tensor(n, s, d))
        mksem = lambda n: Sem(ctx.enter_context(nc.semaphore(n)))

        ab_scale = sb("ab_scale", [128, NCH], F32)
        ab_bias = sb("ab_bias", [128, NCH], F32)
        ident_sb = sb("ident_sb", [128, 128], BF16)
        ubias = sb("ubias", [128, 1], F32)
        zbias = sb("zbias", [128, 1], F32)
        epsb = sb("epsb", [128, 1], F32)
        ones1 = sb("ones1", [1, 128], F32)

        sT = mksem("sT")
        sV = mksem("sV")
        sC = mksem("sC")
        sP = mksem("sP")
        sK0 = mksem("sK0")
        sWPL = [mksem(f"sWPL{i}") for i in range(3)]
        sWPS = [mksem(f"sWPS{i}") for i in range(2)]
        sFX = [mksem(f"sFX{i}") for i in range(2)]
        sWT = [mksem(f"sWT{i}") for i in range(2)]
        sGW = mksem("sGW")
        sK2 = mksem("sK2")
        sK2S = mksem("sK2S")
        sXA = [mksem(f"sXA{i}") for i in range(4)]
        sGR = [mksem(f"sGR{i}") for i in range(2)]
        sXW = [mksem(f"sXW{i}") for i in range(2)]
        sOUT = [mksem(f"sOUT{i}") for i in range(2)]

        c01 = ctx.enter_context(ExitStack())
        wdT_sb = c01.enter_context(nc.sbuf_tensor("wdT_sb", [128, 24 * E], BF16))
        wdT_sb3 = wdT_sb[:].rearrange("p (a b) -> p a b", b=E)
        with ExitStack() as c0:
            sb0 = lambda n, s, d: c0.enter_context(nc.sbuf_tensor(n, s, d))
            ps0 = lambda n, s, d: c0.enter_context(nc.psum_tensor(n, s, d))
            wrow = [sb0(f"wrow{i}", [128, HID], BF16) for i in range(3)]
            stag = [sb0(f"stag{i}", [128, 8, 128], BF16) for i in range(2)]
            pstr = [ps0(f"pstr{i}", [128, 128], BF16) for i in range(6)]

            em = Em()

            em.add("vector", lambda e: e.memset(ubias[:], -U_SHIFT))
            em.add("vector", lambda e: e.memset(zbias[:], 0.0))
            em.add("vector", lambda e: e.memset(epsb[:], EPS))
            em.add("vector", lambda e: e.memset(ones1[:], 1.0).then_inc(sV.h, 1))
            consts_done = sV.inc()
            em.add("gpsimd", lambda e: e.dma_start(out=ident_sb[:], in_=iden).then_inc(sK0.h, 16))
            ident_ld = sK0.inc(16)

            prep_trel = {}
            prep_srel = {}
            trevac = {}
            evn = [0]
            em.wait("scalar", (sV.h, consts_done))

            def prep_matrix(src, n_m, base_pi, store_cb=None, dst_cb=None):
                for m in range(n_m):
                    pi = base_pi + m
                    buf = wrow[pi % 3]
                    stg = stag[pi % 2]
                    wbytes = src.shape[1]
                    if pi >= 3:
                        em.wait("gpsimd", (sT.h, prep_trel[pi - 3]))
                    em.add("gpsimd", lambda e, b=buf, s=src, m=m, w=wbytes, q=pi % 3:
                           e.dma_start(out=b[:, 0:w], in_=s[m * 128:(m + 1) * 128, :])
                           .then_inc(sWPL[q].h, 16))
                    ld = (sWPL[pi % 3].h, sWPL[pi % 3].inc(16))
                    em.wait("tensor", ld)
                    if pi == 0:
                        em.wait("tensor", (sK0.h, ident_ld))
                    nk = src.shape[1] // 128
                    war_stg = prep_srel.get(pi - 2) if store_cb is not None else None
                    for k in range(nk):
                        bank = pstr[evn[0] % 6]
                        if evn[0] >= 6:
                            em.wait("tensor", trevac[evn[0] - 6])
                        em.add("tensor", lambda e, b=bank, w=buf, k=k:
                               e.transpose(b[:], w[:, k * 128:(k + 1) * 128],
                                           ident_sb[:]).then_inc(sT.h, 1))
                        tc = sT.inc()
                        dst = stg[:, k, :] if store_cb is not None else dst_cb(m, k)
                        if evn[0] % 2 == 0:
                            em.wait("vector", (sT.h, tc))
                            if k <= 1 and war_stg is not None:
                                em.wait("vector", war_stg)
                            em.add("vector", lambda e, b=bank, d=dst:
                                   e.tensor_copy(d, b[:]).then_inc(sV.h, 1))
                            trevac[evn[0]] = (sV.h, sV.inc())
                        else:
                            em.wait("scalar", (sT.h, tc))
                            if k <= 1 and war_stg is not None:
                                em.wait("scalar", war_stg)
                            em.add("scalar", lambda e, b=bank, d=dst:
                                   e.activation(d, b[:], AF.Identity, bias=zbias[:],
                                                scale=1.0).then_inc(sC.h, 1))
                            trevac[evn[0]] = (sC.h, sC.inc())
                        evn[0] += 1
                    prep_trel[pi] = sT.n
                    if store_cb is not None:
                        em.wait("sync", trevac[evn[0] - 1])
                        em.wait("sync", trevac[evn[0] - 2])
                        em.add("sync", lambda e, s=stg, m=m, cb=store_cb, q=pi % 2:
                               e.dma_start(out=cb(m), in_=s[:]).then_inc(sWPS[q].h, 16))
                        prep_srel[pi] = (sWPS[pi % 2].h, sWPS[pi % 2].inc(16))

            prep_matrix(wg_i, 24, 0,
                        store_cb=lambda m: scr_gu[m * 128:(m + 1) * 128, 0:1024])
            prep_matrix(wu_i, 24, 24,
                        store_cb=lambda m: scr_gu[m * 128:(m + 1) * 128, 1024:2048])
            prep_matrix(wd_i, 8, 48,
                        dst_cb=lambda m, k: wdT_sb3[:, k, m * 128:(m + 1) * 128])

            b0_wps = (sWPS[0].n, sWPS[1].n)
            with nc.Block("prep") as blk:
                em.run(blk)

        if stages >= 2:
          with ExitStack() as c1:
            sb1 = lambda n, s, d: c1.enter_context(nc.sbuf_tensor(n, s, d))
            ps1 = lambda n, s, d: c1.enter_context(nc.psum_tensor(n, s, d))
            xc1 = [sb1(f"xc1_{i}", [128, E], F32) for i in range(2)]
            bnb = sb1("bnb", [128, 2, 12], F32)
            mvb = sb1("mvb", [128, 2, 2], F32)
            lnc = sb1("lnc", [128, 2], F32)
            tmpc = sb1("tmpc", [128, 2], F32)
            xn_tm = sb1("xn_tm", [128, CPG, E], BF16)
            xn_fm = [sb1(f"xn_fm{i}", [128, 8, TGRP], BF16) for i in range(2)]
            wgut = [sb1(f"wgut{i}", [128, 16, 128], BF16) for i in range(2)]
            silu_g = [sb1(f"silu_g{i}", [128, TGRP], BF16) for i in range(2)]
            h_sb = sb1("h_sb", [128, 24, TGRP], BF16)
            g_stage = [sb1(f"g_stage{i}", [128, E], F32) for i in range(2)]
            ps_xt = [ps1(f"ps_xt{i}", [128, 128], BF16) for i in range(2)]
            ps_gu = [ps1(f"ps_gu{i}", [128, TGRP], F32) for i in range(4)]
            ps_dn = [ps1(f"ps_dn{i}", [128, TGRP], F32) for i in range(2)]

            em = Em()
            em.wait("sync", (sWPS[0].h, b0_wps[0]))
            em.wait("sync", (sWPS[1].h, b0_wps[1]))
            em.wait("sync", (sWPL[0].h, sWPL[0].n))
            em.wait("sync", (sWPL[1].h, sWPL[1].n))
            em.wait("sync", (sWPL[2].h, sWPL[2].n))
            em.wait("sync", (sK0.h, sK0.n))

            xe_done = {}
            ln1_done = {}

            def chunk_stats(ch, par, buf, ld):
                em.wait("vector", ld)
                if ch - 2 in ln1_done:
                    em.wait("vector", (sC.h, ln1_done[ch - 2]))
                em.add("vector", lambda e, b=buf, par=par:
                       e.bn_stats(bnb[:, par, 0:6], b[:, 0:512]))
                em.add("vector", lambda e, b=buf, par=par:
                       e.bn_stats(bnb[:, par, 6:12], b[:, 512:1024]).then_inc(sV.h, 1))
                em.wait("vector", (sV.h, sV.inc()))
                em.add("vector", lambda e, par=par:
                       e.bn_aggr(mvb[:, par, :], bnb[:, par, :]).then_inc(sV.h, 1))
                aggr = sV.inc()
                em.wait("scalar", (sV.h, aggr))
                em.add("scalar", lambda e, par=par:
                       e.activation(lnc[:, par:par + 1], mvb[:, par, 1:2], AF.Ln,
                                    bias=epsb[:], scale=1.0).then_inc(sC.h, 1))
                ln1_done[ch] = sC.inc()
                em.wait("scalar", (sC.h, ln1_done[ch]))
                em.add("scalar", lambda e, ch=ch, par=par:
                       e.activation(ab_scale[:, ch:ch + 1], lnc[:, par:par + 1], AF.Exp,
                                    bias=zbias[:], scale=-0.5).then_inc(sC.h, 1))
                rstd = sC.inc()
                em.wait("vector", (sC.h, rstd))
                em.add("vector", lambda e, ch=ch, par=par:
                       e.tensor_tensor(tmpc[:, par:par + 1], mvb[:, par, 0:1],
                                       ab_scale[:, ch:ch + 1], ALU.mult).then_inc(sV.h, 1))
                em.wait("vector", (sV.h, sV.inc()))
                em.add("vector", lambda e, ch=ch, par=par:
                       e.tensor_scalar(ab_bias[:, ch:ch + 1], tmpc[:, par:par + 1],
                                       -1.0, None, ALU.mult).then_inc(sV.h, 1))
                return sV.inc()
            mm_done = {}
            sg_done = {}
            h_done = {}
            dn_done = {}
            g_done = {}
            wld = {}
            wrel = {}
            xtrel = {}
            xt_last = {}
            gt = [0]

            def emit_xn_chunk(grp, cc):
                ch = grp * CPG + cc
                par = ch % 2
                buf = xc1[par]
                if ch >= 2:
                    em.wait("sync", (sC.h, xe_done[ch - 2]))
                em.add("sync", lambda e, b=buf, ch=ch, par=par:
                       e.dma_start(out=b[:], in_=x_ffn[ch * 128:(ch + 1) * 128, :])
                       .then_inc(sFX[par].h, 16))
                ld = (sFX[par].h, sFX[par].inc(16))
                bias_ok = chunk_stats(ch, par, buf, ld)
                em.wait("scalar", ld)
                em.wait("scalar", (sV.h, bias_ok))
                if ch >= 4:
                    em.wait("scalar", (sV.h, xtrel[(ch - 4) * 8 + 7]))
                em.add("scalar", lambda e, b=buf, cc=cc, ch=ch:
                       e.activation(xn_tm[:, cc, :], b[:], AF.Identity,
                                    bias=ab_bias[:, ch:ch + 1],
                                    scale=ab_scale[:, ch:ch + 1]).then_inc(sC.h, 1))
                xe_done[ch] = sC.inc()

            def emit_transpose_pair(grp, t2):
                fm = xn_fm[grp % 2]
                for t in (t2, t2 + 1):
                    cc, k = divmod(t, 8)
                    g = gt[0]
                    j = g % 2
                    if t % 8 == 0:
                        em.wait("tensor", (sC.h, xe_done[grp * CPG + cc]))
                    if g >= 2:
                        em.wait("tensor", (sV.h, xtrel[g - 2]))
                    em.add("tensor", lambda e, cc=cc, k=k, j=j:
                           e.transpose(ps_xt[j][:], xn_tm[:, cc, k * 128:(k + 1) * 128],
                                       ident_sb[:]).then_inc(sT.h, 1))
                    tc = sT.inc()
                    em.wait("vector", (sT.h, tc))
                    em.add("vector", lambda e, fm=fm, cc=cc, k=k, j=j:
                           e.tensor_copy(fm[:, k, cc * 128:(cc + 1) * 128],
                                         ps_xt[j][:]).then_inc(sV.h, 1))
                    xtrel[g] = sV.inc()
                    gt[0] += 1
                if t2 + 2 == 32:
                    xt_last[grp] = sV.n

            for cc in range(CPG):
                emit_xn_chunk(0, cc)
            for t2 in range(0, 32, 2):
                emit_transpose_pair(0, t2)

            for grp in range(NGRP):
                fm = xn_fm[grp % 2]
                for m in range(24):
                    if grp + 1 < NGRP:
                        if m % 2 == 0 and m < 8:
                            emit_xn_chunk(grp + 1, m // 2)
                        if m >= 8:
                            emit_transpose_pair(grp + 1, (m - 8) * 2)
                    par = m % 2
                    pkey = (grp, m - 2) if m >= 2 else ((grp - 1, 22 + m) if grp > 0 else None)
                    if pkey is not None:
                        em.wait("sync", (sT.h, wrel[pkey]))
                    em.add("sync", lambda e, m=m, par=par:
                           e.dma_start(out=wgut[par][:],
                                       in_=scr_gu[m * 128:(m + 1) * 128, :])
                           .then_inc(sWT[par].h, 16))
                    wld[(grp, m)] = (sWT[par].h, sWT[par].inc(16))
                    em.wait("tensor", wld[(grp, m)])
                    if m == 0:
                        em.wait("tensor", (sV.h, xt_last[grp]))
                    if pkey is not None:
                        em.wait("tensor", (sC.h, sg_done[pkey]))
                        em.wait("tensor", (sV.h, h_done[pkey]))
                    if m <= 1 and grp > 0:
                        em.wait("tensor", (sC.h, g_done[(grp - 1, 3)]))
                    wt = wgut[par]
                    for k in range(8):
                        em.add("tensor", lambda e, wt=wt, k=k, par=par, fm=fm:
                               e.matmul(ps_gu[par][:], wt[:, k, :], fm[:, k, :],
                                        start=(k == 0), stop=(k == 7)))
                    for k in range(8):
                        last = k == 7
                        em.add("tensor", lambda e, wt=wt, k=k, par=par, fm=fm, last=last:
                               (e.matmul(ps_gu[2 + par][:], wt[:, 8 + k, :], fm[:, k, :],
                                         start=(k == 0), stop=last).then_inc(sT.h, 1)
                                if last else
                                e.matmul(ps_gu[2 + par][:], wt[:, 8 + k, :], fm[:, k, :],
                                         start=(k == 0), stop=last)))
                    mm_done[(grp, m)] = sT.inc()
                    wrel[(grp, m)] = mm_done[(grp, m)]
                    em.wait("scalar", (sT.h, mm_done[(grp, m)]))
                    if pkey is not None:
                        em.wait("scalar", (sV.h, h_done[pkey]))
                    em.add("scalar", lambda e, par=par:
                           e.activation(silu_g[par][:], ps_gu[par][:], AF.Silu,
                                        bias=zbias[:], scale=INV127).then_inc(sC.h, 1))
                    sg_done[(grp, m)] = sC.inc()
                    em.wait("vector", (sC.h, sg_done[(grp, m)]))
                    em.add("vector", lambda e, m=m, par=par:
                           e.tensor_tensor(h_sb[:, m, :], ps_gu[2 + par][:], silu_g[par][:],
                                           ALU.mult).then_inc(sV.h, 1))
                    h_done[(grp, m)] = sV.inc()

                for tt in range(CPG):
                    par = tt % 2
                    pair = (ps_dn[0], ps_dn[1]) if tt % 2 == 0 else (ps_gu[0], ps_gu[1])
                    if tt >= 2:
                        em.wait("tensor", (sC.h, g_done[(grp, tt - 2)]))
                    elif tt == 0:
                        if grp > 0:
                            em.wait("tensor", (sC.h, g_done[(grp - 1, 2)]))
                        em.wait("tensor", (sV.h, h_done[(grp, 23)]))
                    else:
                        em.wait("tensor", (sC.h, sg_done[(grp, 23)]))
                    for half in range(2):
                        for k in range(24):
                            last = (k == 23 and half == 1)
                            em.add("tensor", lambda e, tt=tt, k=k, half=half,
                                   pair=pair, last=last:
                                   (e.matmul(pair[half][:], h_sb[:, k, tt * 128:(tt + 1) * 128],
                                             wdT_sb3[:, k, half * 512:(half + 1) * 512],
                                             start=(k == 0), stop=(k == 23)).then_inc(sT.h, 1)
                                    if last else
                                    e.matmul(pair[half][:], h_sb[:, k, tt * 128:(tt + 1) * 128],
                                             wdT_sb3[:, k, half * 512:(half + 1) * 512],
                                             start=(k == 0), stop=(k == 23))))
                    dn_done[(grp, tt)] = sT.inc()
                    em.wait("scalar", (sT.h, dn_done[(grp, tt)]))
                    if sGW.n >= 32:
                        em.wait("scalar", (sGW.h, sGW.n))
                    for half in range(2):
                        last = half == 1
                        em.add("scalar", lambda e, half=half, par=par, pair=pair, last=last:
                               e.activation(g_stage[par][:, half * 512:(half + 1) * 512],
                                            pair[half][:], AF.Silu,
                                            bias=zbias[:], scale=INV127SQ).then_inc(sC.h, 1))
                        sC.inc()
                    g_done[(grp, tt)] = sC.n
                    em.wait("scalar", (sC.h, g_done[(grp, tt)]))
                    r0 = (grp * CPG + tt) * 128
                    em.add("scalar", lambda e, r0=r0, par=par:
                           e.dma_start(out=scr_g[r0:r0 + 128, :],
                                       in_=g_stage[par][:]).then_inc(sGW.h, 16))
                    sGW.inc(16)

            b1_gw = sGW.n
            with nc.Block("ffn") as blk:
                em.run(blk)
        c01.close()

        if stages >= 3:
          with ExitStack() as c2:
            sb2 = lambda n, s, d: c2.enter_context(nc.sbuf_tensor(n, s, d))
            q1r = sb2("q1r", [128, 8, AW], BF16)
            q2r = sb2("q2r", [128, 8, AW], BF16)
            k1r = sb2("k1r", [128, AW], BF16)
            k2r = sb2("k2r", [128, AW], BF16)
            v0_tm = sb2("v0_tm", [128, ACH, 128], BF16)
            v1_tm = sb2("v1_tm", [128, ACH, 128], BF16)
            lamt = sb2("lamt", [128, 8], F32)

            em = Em()
            for eng in ("sync", "gpsimd"):
                em.wait(eng, (sGW.h, b1_gw))
                em.wait(eng, (sWT[0].h, sWT[0].n))
                em.wait(eng, (sWT[1].h, sWT[1].n))
                em.wait(eng, (sFX[0].h, sFX[0].n))
                em.wait(eng, (sFX[1].h, sFX[1].n))

            em.add("vector", lambda e: e.memset(v0_tm[:], 0.0))
            em.add("vector", lambda e: e.memset(v1_tm[:], 0.0).then_inc(sV.h, 1))
            em.wait("vector", (sV.h, sV.inc()))
            em.add("vector", lambda e: e.memset(v0_tm[:, :, 64:65], 1.0))
            em.add("vector", lambda e: e.memset(v1_tm[:, :, 0:1], 1.0).then_inc(sV.h, 1))
            vinit_done = sV.inc()

            with ExitStack() as cp:
                sbp = lambda n, s, d: cp.enter_context(nc.sbuf_tensor(n, s, d))
                psp = lambda n, s, d: cp.enter_context(nc.psum_tensor(n, s, d))
                wq1_sb = sbp("wq1_sb", [128, 8, E], BF16)
                wq2_sb = sbp("wq2_sb", [128, 8, E], BF16)
                wk1_sb = sbp("wk1_sb", [128, 8, 128], BF16)
                wk2_sb = sbp("wk2_sb", [128, 8, 128], BF16)
                wv_sb = sbp("wv_sb", [128, 8, 128], BF16)
                paT_sb = sbp("paT_sb", [128, 128], BF16)
                pbT_sb = sbp("pbT_sb", [128, 128], BF16)
                cos_sb = sbp("cos_sb", [128, AW], F32)
                sin_sb = sbp("sin_sb", [128, AW], F32)
                xna_fm = sbp("xna_fm", [128, 8, AW], BF16)
                qf = sbp("qf", [128, 2, 512], BF16)
                rt1 = sbp("rt1", [128, 512], BF16)
                rt2 = sbp("rt2", [128, 512], BF16)
                lv_sb = sbp("lv_sb", [1, 256], F32)
                dots = sbp("dots", [1, 130], F32)
                ps_p = [psp(f"ps_p{i}", [128, 512], F32) for i in range(2)]
                ps_ra = psp("ps_ra", [128, 512], F32)
                ps_rb = psp("ps_rb", [128, 512], F32)
                ps_vt = [psp(f"ps_vt{i}", [128, 128], BF16) for i in range(2)]
                ps_lam = psp("ps_lam", [128, 2], F32)

                for t_ap, t_sb in [(wq1, wq1_sb), (wq2, wq2_sb), (wk1, wk1_sb),
                                   (wk2, wk2_sb), (wv, wv_sb)]:
                    em.add("gpsimd", lambda e, a=t_ap, s=t_sb:
                           e.dma_start(out=s[:], in_=a.rearrange("(k p) n -> p k n", p=128))
                           .then_inc(sK2.h, 16))
                    sK2.inc(16)
                em.add("gpsimd", lambda e: e.dma_start(out=paT_sb[:], in_=paT).then_inc(sK2.h, 16))
                em.add("gpsimd", lambda e: e.dma_start(out=pbT_sb[:], in_=pbT).then_inc(sK2.h, 16))
                em.add("gpsimd", lambda e: e.dma_start(out=cos_sb[:], in_=tcos).then_inc(sK2.h, 16))
                em.add("gpsimd", lambda e: e.dma_start(out=sin_sb[:], in_=tsin).then_inc(sK2.h, 16))
                sK2.inc(16 * 4)
                em.add("sync", lambda e: e.dma_start(out=lv_sb[:], in_=lvec).then_inc(sK2S.h, 16))
                lv_ld = sK2S.inc(16)
                wload = sK2.n

                em.wait("vector", (sK2S.h, lv_ld))
                em.add("vector", lambda e: e.tensor_tensor(dots[:, 0:64], lv_sb[:, 0:64],
                                                           lv_sb[:, 64:128], ALU.mult))
                em.add("vector", lambda e: e.tensor_tensor(dots[:, 64:128], lv_sb[:, 128:192],
                                                           lv_sb[:, 192:256], ALU.mult).then_inc(sV.h, 1))
                dmul = sV.inc()
                em.wait("vector", (sV.h, dmul))
                em.add("vector", lambda e: e.tensor_reduce(dots[:, 128:129], dots[:, 0:64], AX.X, ALU.add))
                em.add("vector", lambda e: e.tensor_reduce(dots[:, 129:130], dots[:, 64:128],
                                                           AX.X, ALU.add).then_inc(sV.h, 1))
                dred = sV.inc()
                em.wait("tensor", (sV.h, dred))
                em.add("tensor", lambda e: e.matmul(ps_lam[:], ones1[:], dots[:, 128:130],
                                                    start=True, stop=True).then_inc(sT.h, 1))
                dbc = sT.inc()
                em.wait("scalar", (sT.h, dbc))
                em.add("scalar", lambda e: e.activation(lamt[:, 2:4], ps_lam[:], AF.Exp,
                                                        bias=zbias[:], scale=1.0).then_inc(sC.h, 1))
                lexp = sC.inc()
                em.wait("vector", (sC.h, lexp))
                em.add("vector", lambda e: e.tensor_tensor(lamt[:, 4:5], lamt[:, 2:3],
                                                           lamt[:, 3:4], ALU.subtract).then_inc(sV.h, 1))
                em.wait("vector", (sV.h, sV.inc()))
                em.add("vector", lambda e: e.tensor_scalar(lamt[:, 5:6], lamt[:, 4:5], 0.8, None,
                                                           ALU.add).then_inc(sV.h, 1))
                em.wait("vector", (sV.h, sV.inc()))
                em.add("vector", lambda e: e.tensor_scalar(lamt[:, 6:7], lamt[:, 5:6], 0.1, 0.9,
                                                           ALU.max, ALU.min).then_inc(sV.h, 1))
                em.wait("vector", (sV.h, sV.inc()))
                em.add("vector", lambda e: e.tensor_scalar(lamt[:, 1:2], lamt[:, 6:7], -1.0, None,
                                                           ALU.mult).then_inc(sV.h, 1))
                lam_done = sV.inc()

                with ExitStack() as cx:
                    xa_tm = cx.enter_context(nc.sbuf_tensor("xa_tm", [128, 4, E], BF16))
                    xa_f32 = [cx.enter_context(nc.sbuf_tensor(f"xa_f32_{i}", [128, E], F32))
                              for i in range(4)]
                    bnb2 = cx.enter_context(nc.sbuf_tensor("bnb2", [128, 2, 12], F32))
                    mvb2 = cx.enter_context(nc.sbuf_tensor("mvb2", [128, 2, 2], F32))
                    lnc2 = cx.enter_context(nc.sbuf_tensor("lnc2", [128, 2], F32))
                    tmpc2 = cx.enter_context(nc.sbuf_tensor("tmpc2", [128, 2], F32))
                    ln2_done = {}

                    def xa_stats(ch, par, buf, ld):
                        em.wait("vector", ld)
                        if ch - 2 in ln2_done:
                            em.wait("vector", (sC.h, ln2_done[ch - 2]))
                        em.add("vector", lambda e, b=buf, par=par:
                               e.bn_stats(bnb2[:, par, 0:6], b[:, 0:512]))
                        em.add("vector", lambda e, b=buf, par=par:
                               e.bn_stats(bnb2[:, par, 6:12], b[:, 512:1024]).then_inc(sV.h, 1))
                        em.wait("vector", (sV.h, sV.inc()))
                        em.add("vector", lambda e, par=par:
                               e.bn_aggr(mvb2[:, par, :], bnb2[:, par, :]).then_inc(sV.h, 1))
                        aggr = sV.inc()
                        em.wait("scalar", (sV.h, aggr))
                        em.add("scalar", lambda e, par=par:
                               e.activation(lnc2[:, par:par + 1], mvb2[:, par, 1:2], AF.Ln,
                                            bias=epsb[:], scale=1.0).then_inc(sC.h, 1))
                        ln2_done[ch] = sC.inc()
                        em.wait("scalar", (sC.h, ln2_done[ch]))
                        em.add("scalar", lambda e, ch=ch, par=par:
                               e.activation(ab_scale[:, ch:ch + 1], lnc2[:, par:par + 1],
                                            AF.Exp, bias=zbias[:], scale=-0.5).then_inc(sC.h, 1))
                        rstd = sC.inc()
                        em.wait("vector", (sC.h, rstd))
                        em.add("vector", lambda e, ch=ch, par=par:
                               e.tensor_tensor(tmpc2[:, par:par + 1], mvb2[:, par, 0:1],
                                               ab_scale[:, ch:ch + 1], ALU.mult).then_inc(sV.h, 1))
                        em.wait("vector", (sV.h, sV.inc()))
                        em.add("vector", lambda e, ch=ch, par=par:
                               e.tensor_scalar(ab_bias[:, ch:ch + 1], tmpc2[:, par:par + 1],
                                               -1.0, None, ALU.mult).then_inc(sV.h, 1))
                        return sV.inc()

                    xa_done = {}
                    xat_done = {}
                    xarel = {}
                    for i in range(ACH):
                        par = i % 4
                        buf = xa_f32[par]
                        if i >= 4:
                            em.wait("sync", (sC.h, xa_done[i - 4]))
                        em.add("sync", lambda e, b=buf, i=i, par=par:
                               e.dma_start(out=b[:], in_=x_attn[i * 128:(i + 1) * 128, :])
                               .then_inc(sXA[par].h, 16))
                        ld = (sXA[par].h, sXA[par].inc(16))
                        ch = PCH + i
                        bias_ok = xa_stats(ch, i % 2, buf, ld)
                        em.wait("scalar", ld)
                        em.wait("scalar", (sV.h, bias_ok))
                        if i >= 4:
                            em.wait("scalar", (sT.h, xat_done[i - 4]))
                        em.add("scalar", lambda e, b=buf, i=i, ch=ch:
                               e.activation(xa_tm[:, i % 4, :], b[:], AF.Identity,
                                            bias=ab_bias[:, ch:ch + 1],
                                            scale=ab_scale[:, ch:ch + 1]).then_inc(sC.h, 1))
                        xa_done[i] = sC.inc()
                        em.wait("tensor", (sC.h, xa_done[i]))
                        for k in range(8):
                            t = i * 8 + k
                            j = t % 2
                            if t >= 2:
                                em.wait("tensor", xarel[t - 2])
                            em.add("tensor", lambda e, i=i, k=k, j=j:
                                   e.transpose(ps_vt[j][:], xa_tm[:, i % 4, k * 128:(k + 1) * 128],
                                               ident_sb[:]).then_inc(sT.h, 1))
                            tc = sT.inc()
                            if t % 2 == 0:
                                em.wait("vector", (sT.h, tc))
                                em.add("vector", lambda e, i=i, k=k, j=j:
                                       e.tensor_copy(xna_fm[:, k, i * 128:(i + 1) * 128],
                                                     ps_vt[j][:]).then_inc(sV.h, 1))
                                xarel[t] = (sV.h, sV.inc())
                            else:
                                em.wait("scalar", (sT.h, tc))
                                em.add("scalar", lambda e, i=i, k=k, j=j:
                                       e.activation(xna_fm[:, k, i * 128:(i + 1) * 128],
                                                    ps_vt[j][:], AF.Identity, bias=zbias[:],
                                                    scale=1.0).then_inc(sC.h, 1))
                                xarel[t] = (sC.h, sC.inc())
                        xat_done[i] = sT.n
                    xna_ready = (sV.n, sC.n)

                    proj_i = [0]
                    psem = {}
                    qf_rel = [None]
                    rope_rel = [None]
                    rope_pool = [None]
                    vt_rel = {0: None, 1: None}

                    def project(w_sb, nm, dest, rope):
                        for m in range(nm):
                            for (tb, tn) in TT_SPLITS:
                                it = proj_i[0]
                                sl = it % 2
                                bank = ps_p[it % 2]
                                if psem.get(it % 2) is not None:
                                    em.wait("tensor", (sC.h, psem[it % 2]))
                                if it == 0:
                                    em.wait("tensor", (sK2.h, wload))
                                    em.wait("tensor", (sV.h, xna_ready[0]))
                                    em.wait("tensor", (sC.h, xna_ready[1]))
                                for k in range(8):
                                    last = k == 7
                                    em.add("tensor", lambda e, b=bank, w=w_sb, m=m, k=k, tb=tb,
                                           tn=tn, last=last:
                                           (e.matmul(b[:, 0:tn], w[:, k, m * 128:(m + 1) * 128],
                                                     xna_fm[:, k, tb:tb + tn],
                                                     start=(k == 0), stop=last).then_inc(sT.h, 1)
                                            if last else
                                            e.matmul(b[:, 0:tn], w[:, k, m * 128:(m + 1) * 128],
                                                     xna_fm[:, k, tb:tb + tn],
                                                     start=(k == 0), stop=last)))
                                mmv = sT.inc()
                                em.wait("scalar", (sT.h, mmv))
                                if qf_rel[0] is not None:
                                    em.wait("scalar", (sT.h, qf_rel[0]))
                                em.add("scalar", lambda e, b=bank, tn=tn, sl=sl:
                                       e.activation(qf[:, sl, 0:tn], b[:, 0:tn], AF.Identity,
                                                    bias=zbias[:], scale=1.0).then_inc(sC.h, 1))
                                psem[it % 2] = sC.inc()
                                if not rope:
                                    em.wait("tensor", (sC.h, psem[it % 2]))
                                    for jj in range(tn // 128):
                                        tch = tb // 128 + jj
                                        vb = tch % 2
                                        if vt_rel[vb] is not None:
                                            em.wait("tensor", (sV.h, vt_rel[vb]))
                                        em.add("tensor", lambda e, jj=jj, sl=sl, vb=vb:
                                               e.transpose(ps_vt[vb][:], qf[:, sl, jj * 128:(jj + 1) * 128],
                                                           ident_sb[:]).then_inc(sT.h, 1))
                                        tc2 = sT.inc()
                                        em.wait("vector", (sT.h, tc2))
                                        if tch == 0:
                                            em.wait("vector", (sV.h, vinit_done))
                                        em.add("vector", lambda e, tch=tch, vb=vb:
                                               e.tensor_copy(v0_tm[:, tch, 0:64], ps_vt[vb][:, 0:64]))
                                        em.add("vector", lambda e, tch=tch, vb=vb:
                                               e.tensor_copy(v1_tm[:, tch, 64:128],
                                                             ps_vt[vb][:, 64:128]).then_inc(sV.h, 1))
                                        vt_rel[vb] = sV.inc()
                                    qf_rel[0] = sT.n
                                else:
                                    em.wait("tensor", (sC.h, psem[it % 2]))
                                    if rope_rel[0] is not None:
                                        em.wait("tensor", (sV.h, rope_rel[0]))
                                    em.add("tensor", lambda e, tn=tn, sl=sl:
                                           e.matmul(ps_ra[:, 0:tn], paT_sb[:], qf[:, sl, 0:tn],
                                                    start=True, stop=True))
                                    em.add("tensor", lambda e, tn=tn, sl=sl:
                                           e.matmul(ps_rb[:, 0:tn], pbT_sb[:], qf[:, sl, 0:tn],
                                                    start=True, stop=True).then_inc(sT.h, 1))
                                    qf_rel[0] = sT.inc()
                                    em.wait("vector", (sT.h, qf_rel[0]))
                                    if rope_pool[0] is not None:
                                        em.wait("vector", (sP.h, rope_pool[0]))
                                    em.add("vector", lambda e, tb=tb, tn=tn:
                                           e.tensor_tensor(rt1[:, 0:tn], ps_ra[:, 0:tn],
                                                           cos_sb[:, tb:tb + tn], ALU.mult))
                                    em.add("vector", lambda e, tb=tb, tn=tn:
                                           e.tensor_tensor(rt2[:, 0:tn], ps_rb[:, 0:tn],
                                                           sin_sb[:, tb:tb + tn], ALU.mult).then_inc(sV.h, 1))
                                    rope_rel[0] = sV.inc()
                                    em.wait("gpsimd", (sV.h, rope_rel[0]))
                                    dst = dest(m, tb, tn)
                                    em.add("gpsimd", lambda e, d=dst, tn=tn:
                                           e.tensor_tensor(d, rt1[:, 0:tn], rt2[:, 0:tn],
                                                           ALU.add).then_inc(sP.h, 1))
                                    rope_pool[0] = sP.inc()
                                proj_i[0] += 1

                    project(wq1_sb, 8, lambda m, tb, tn: q1r[:, m, tb:tb + tn], True)
                    project(wq2_sb, 8, lambda m, tb, tn: q2r[:, m, tb:tb + tn], True)
                    project(wk1_sb, 1, lambda m, tb, tn: k1r[:, tb:tb + tn], True)
                    project(wk2_sb, 1, lambda m, tb, tn: k2r[:, tb:tb + tn], True)
                    project(wv_sb, 1, None, False)
                    em.wait("tensor", (sP.h, sP.n))
                    em.wait("tensor", (sV.h, sV.n))
            proj_end = (sT.n, sV.n, sC.n, sP.n)

            with ExitStack() as cw:
                sbw = lambda n, s, d: cw.enter_context(nc.sbuf_tensor(n, s, d))
                psw = lambda n, s, d: cw.enter_context(nc.psum_tensor(n, s, d))
                wout_sb = sbw("wout_sb", [128, 8, E], BF16)
                bout_sb = sbw("bout_sb", [1, E], F32)
                pe_sb = [sbw(f"pe_sb{p}", [128, 2, 512], BF16) for p in range(2)]
                rz32_sb = sbw("rz32_sb", [128, 2, 512], F32)
                rz_sb = sbw("rz_sb", [128, 2, 512], BF16)
                asb_e = [sbw(f"asb_e{i}", [128, 512], BF16) for i in range(2)]
                tcmb = [sbw(f"tcmb{i}", [128, 512], BF16) for i in range(2)]
                onesq = sbw("onesq", [128, 128], BF16)
                lamt_bf = sbw("lamt_bf", [128, 1], BF16)
                a_sb = [sbw(f"a_sb{i}", [128, 8, 256], BF16) for i in range(2)]
                g_tm = [sbw(f"g_tm{i}", [128, 2, E], F32) for i in range(2)]
                x_win = [sbw(f"x_win{i}", [128, 2, E], F32) for i in range(2)]
                o_stage = [sbw(f"o_stage{i}", [128, 2, E], F32) for i in range(2)]
                prod = sbw("prod", [128, 512], F32)
                ps_s = [[psw(f"ps_s{hp}{p}", [128, 512], F32) for p in range(2)]
                        for hp in range(2)]
                ps_a = psw("ps_a", [128, 512], F32)
                ps_r = psw("ps_r", [128, 2, 512], F32)
                ps_w = psw("ps_w", [128, 512], F32)

                for eng in ("gpsimd", "sync", "scalar"):
                    em.wait(eng, (sT.h, proj_end[0]))
                    em.wait(eng, (sV.h, proj_end[1]))
                    em.wait(eng, (sC.h, proj_end[2]))
                    em.wait(eng, (sP.h, proj_end[3]))
                em.wait("gpsimd", (sK2.h, wload))
                for q in range(4):
                    em.wait("sync", (sXA[q].h, sXA[q].n))
                em.wait("sync", (sK2S.h, lv_ld))

                em.add("gpsimd", lambda e:
                       e.dma_start(out=wout_sb[:], in_=wout.rearrange("(k p) n -> p k n", p=128))
                       .then_inc(sK2.h, 16))
                sK2.inc(16)
                em.add("sync", lambda e: e.dma_start(out=bout_sb[:], in_=bout).then_inc(sK2S.h, 16))
                bout_ld = sK2S.inc(16)
                wb_ld = sK2.n

                em.add("vector", lambda e: e.memset(onesq[:], 1.0).then_inc(sV.h, 1))
                ones_done = sV.inc()
                em.wait("vector", (sV.h, lam_done))
                em.add("vector", lambda e: e.tensor_copy(lamt_bf[:], lamt[:, 1:2]).then_inc(sV.h, 1))
                lambf_done = sV.inc()

                rel_w = [None]
                rel_prod = [None]
                gx_loaded = {}
                epi_done = {}
                out_cnt = [0, 0]

                def prefetch(n):
                    par = n % 2
                    if n == 0:
                        em.wait("scalar", (sGW.h, b1_gw))
                    if n >= 2:
                        em.wait("scalar", (sV.h, epi_done[n - 2][0]))
                        em.wait("sync", (sP.h, epi_done[n - 2][1]))
                    em.add("scalar", lambda e, n=n, par=par:
                           e.dma_start(out=g_tm[par][:],
                                       in_=scr_g[n * 256:(n + 1) * 256, :]
                                       .rearrange("(a p) e -> p a e", p=128))
                           .then_inc(sGR[par].h, 16))
                    sGR[par].inc(16)
                    em.add("sync", lambda e, n=n, par=par:
                           e.dma_start(out=x_win[par][:],
                                       in_=x_ffn[n * 256:(n + 1) * 256, :]
                                       .rearrange("(a p) e -> p a e", p=128)).then_inc(sXW[par].h, 16))
                    sXW[par].inc(16)
                    gx_loaded[n] = ((sGR[par].h, sGR[par].n), (sXW[par].h, sXW[par].n))

                prefetch(0)
                prefetch(1)

                NHH = NW * H
                s1_done = {}
                s2p = {}
                s6_done = {}
                recip_done = {}
                evac_done = {}
                bcast_done = {}
                comb_done = {}

                def emit_scores_pair(s):
                    for p in range(2):
                        qbuf = q1r if p == 0 else q2r
                        kbuf = k1r if p == 0 else k2r
                        for x in (s, s + 1):
                            if x >= 2:
                                em.wait("tensor", (sC.h, s2p[(x - 2, p)]))
                        for kc in range(2):
                            for x in (s, s + 1):
                                n, h = divmod(x, H)
                                c, po = h // 2, (h % 2) * 64
                                last = kc == 1 and x == s + 1
                                em.add("tensor", lambda e, q=qbuf, k=kbuf, n=n, kc=kc,
                                       po=po, c=c, p=p, hp=x % 2, last=last:
                                       (e.matmul(ps_s[hp][p][:, kc * 256:(kc + 1) * 256],
                                                 k[po:po + 64, 128 * (n + kc):128 * (n + kc) + 128],
                                                 q[po:po + 64, c, 128 * n:128 * n + 256],
                                                 start=(kc == 0), stop=(kc == 1)).then_inc(sT.h, 1)
                                        if last else
                                        e.matmul(ps_s[hp][p][:, kc * 256:(kc + 1) * 256],
                                                 k[po:po + 64, 128 * (n + kc):128 * (n + kc) + 128],
                                                 q[po:po + 64, c, 128 * n:128 * n + 256],
                                                 start=(kc == 0), stop=(kc == 1))))
                        tok = sT.inc()
                        s1_done[(s, p)] = tok
                        s1_done[(s + 1, p)] = tok

                def emit_F(hh):
                    hp = hh % 2
                    if hh % 2 == 0:
                        emit_scores_pair(hh)
                    for p in range(2):
                        em.wait("scalar", (sT.h, s1_done[(hh, p)]))
                        if hh >= 2 and p == 0:
                            em.wait("scalar", (sT.h, s6_done[hh - 2]))
                        em.add("scalar", lambda e, p=p, hp=hp:
                               e.activation(pe_sb[p][:, hp, :], ps_s[hp][p][:], AF.Exp,
                                            bias=ubias[:], scale=0.125).then_inc(sC.h, 1))
                        s2p[(hh, p)] = sC.inc()

                def emit_V(hh):
                    n, h = divmod(hh, H)
                    hp = hh % 2
                    g, po = h % 2, (h % 2) * 64
                    zp = 64 if g == 0 else 0
                    vb = v0_tm if g == 0 else v1_tm
                    mhi = 65 if g == 0 else 128
                    em.wait("tensor", (sC.h, s2p[(hh, 1)]))
                    if hh >= 1:
                        em.wait("tensor", evac_done[hh - 1])
                        em.wait("tensor", recip_done[hh - 1])
                    for p in range(2):
                        for kc in range(2):
                            last = p == 1 and kc == 1
                            em.add("tensor", lambda e, n=n, kc=kc, p=p, hp=hp, vb=vb,
                                   mhi=mhi, last=last:
                                   (e.matmul(ps_a[0:mhi, p * 256:(p + 1) * 256],
                                             vb[:, n + kc, 0:mhi],
                                             pe_sb[p][:, hp, kc * 256:(kc + 1) * 256],
                                             start=(kc == 0), stop=(kc == 1)).then_inc(sT.h, 1)
                                    if last else
                                    e.matmul(ps_a[0:mhi, p * 256:(p + 1) * 256],
                                             vb[:, n + kc, 0:mhi],
                                             pe_sb[p][:, hp, kc * 256:(kc + 1) * 256],
                                             start=(kc == 0), stop=(kc == 1))))
                    s6_done[hh] = sT.inc()
                    rp = (hh // 2) % 2
                    em.wait("scalar", (sT.h, s6_done[hh]))
                    if hh >= 4:
                        em.wait("scalar", (sT.h, bcast_done[hh - 4]))
                    em.add("scalar", lambda e, rp=rp, zp=zp:
                           e.activation(rz32_sb[zp:zp + 1, rp, :], ps_a[zp:zp + 1, :],
                                        AF.Ln, bias=zbias[zp:zp + 1, :],
                                        scale=1.0).then_inc(sC.h, 1))
                    em.wait("scalar", (sC.h, sC.inc()))
                    em.add("scalar", lambda e, rp=rp, zp=zp:
                           e.activation(rz_sb[zp:zp + 1, rp, :], rz32_sb[zp:zp + 1, rp, :],
                                        AF.Exp, bias=zbias[zp:zp + 1, :],
                                        scale=-1.0).then_inc(sC.h, 1))
                    recip_done[hh] = (sC.h, sC.inc())
                    em.wait("vector", (sT.h, s6_done[hh]))
                    if hh >= 2:
                        em.wait("vector", (sV.h, comb_done[hh - 2]))
                    em.add("vector", lambda e, hp=hp, po=po:
                           e.tensor_copy(asb_e[hp][po:po + 64, :],
                                         ps_a[po:po + 64, :]).then_inc(sV.h, 1))
                    evac_done[hh] = (sV.h, sV.inc())

                def emit_R(hh):
                    n, h = divmod(hh, H)
                    hp = hh % 2
                    rp = (hh // 2) % 2
                    c, po = h // 2, (h % 2) * 64
                    zp = 64 if h % 2 == 0 else 0
                    em.wait("tensor", recip_done[hh])
                    if hh == 0:
                        em.wait("tensor", (sV.h, ones_done))
                    if hh >= 2:
                        em.wait("tensor", (sV.h, comb_done[hh - 2]))
                    em.add("tensor", lambda e, hp=hp, rp=rp, zp=zp:
                           e.matmul(ps_r[:, hp, :], onesq[zp:zp + 1, 0:128],
                                    rz_sb[zp:zp + 1, rp, :],
                                    start=True, stop=True).then_inc(sT.h, 1))
                    bcast_done[hh] = sT.inc()
                    em.wait("vector", (sT.h, bcast_done[hh]))
                    em.wait("vector", evac_done[hh])
                    if hh <= 1:
                        em.wait("vector", (sV.h, lambf_done))
                    em.add("vector", lambda e, hp=hp, po=po:
                           e.tensor_tensor(tcmb[hp][po:po + 64, 0:256],
                                           asb_e[hp][po:po + 64, 0:256],
                                           ps_r[po:po + 64, hp, 0:256], ALU.mult))
                    em.add("vector", lambda e, hp=hp, po=po:
                           e.tensor_tensor(tcmb[hp][po:po + 64, 256:512],
                                           asb_e[hp][po:po + 64, 256:512],
                                           ps_r[po:po + 64, hp, 256:512],
                                           ALU.mult).then_inc(sV.h, 1))
                    em.wait("vector", (sV.h, sV.inc()))
                    em.add("vector", lambda e, hp=hp, po=po, c=c, wp=n % 2:
                           e.scalar_tensor_tensor(a_sb[wp][po:po + 64, c, :],
                                                  tcmb[hp][po:po + 64, 256:512],
                                                  lamt_bf[po:po + 64, 0:1],
                                                  tcmb[hp][po:po + 64, 0:256],
                                                  ALU.mult, ALU.add).then_inc(sV.h, 1))
                    comb_done[hh] = sV.inc()

                tail_jobs = []
                tail_epi = {}

                def tail_group(n, sgrp):
                    par = n % 2
                    s, half = [(0, 0), (0, 1), (1, 0), (1, 1)][sgrp]
                    if sgrp == 0:
                        em.wait("tensor", (sV.h, comb_done[n * H + H - 1]))
                        if n == 0:
                            em.wait("tensor", (sK2.h, wb_ld))
                            em.wait("tensor", (sK2S.h, bout_ld))
                    if rel_w[0] is not None:
                        em.wait("tensor", (sV.h, rel_w[0]))
                    em.add("tensor", lambda e, half=half:
                           e.matmul(ps_w[:], ones1[:], bout_sb[:, half * 512:(half + 1) * 512],
                                    start=True, stop=False))
                    for k in range(8):
                        last = k == 7
                        em.add("tensor", lambda e, s=s, half=half, k=k, par=par, last=last:
                               (e.matmul(ps_w[:], a_sb[par][:, k, s * 128:(s + 1) * 128],
                                         wout_sb[:, k, half * 512:(half + 1) * 512],
                                         start=False, stop=last).then_inc(sT.h, 1)
                                if last else
                                e.matmul(ps_w[:], a_sb[par][:, k, s * 128:(s + 1) * 128],
                                         wout_sb[:, k, half * 512:(half + 1) * 512],
                                         start=False, stop=last)))
                    wmm = sT.inc()
                    em.wait("vector", (sT.h, wmm))
                    if sgrp == 0:
                        em.wait("vector", gx_loaded[n][0])
                    if rel_prod[0] is not None:
                        em.wait("vector", (sP.h, rel_prod[0]))
                    em.add("vector", lambda e, s=s, half=half, par=par:
                           e.tensor_tensor(prod[:], ps_w[:],
                                           g_tm[par][:, s, half * 512:(half + 1) * 512],
                                           ALU.mult).then_inc(sV.h, 1))
                    pm = sV.inc()
                    rel_w[0] = pm
                    tail_epi[n][0].append(pm)
                    em.wait("gpsimd", (sV.h, pm))
                    if sgrp == 0:
                        em.wait("gpsimd", gx_loaded[n][1])
                        if n >= 2:
                            em.wait("gpsimd", (sOUT[par].h, 16 * out_cnt[par]))
                    em.add("gpsimd", lambda e, s=s, half=half, par=par:
                           e.tensor_tensor(o_stage[par][:, s, half * 512:(half + 1) * 512], prod[:],
                                           x_win[par][:, s, half * 512:(half + 1) * 512],
                                           ALU.add).then_inc(sP.h, 1))
                    tail_epi[n][1].append(sP.inc())
                    rel_prod[0] = tail_epi[n][1][-1]

                def tail_finish(n):
                    par = n % 2
                    epi_done[n] = (max(tail_epi[n][0]), max(tail_epi[n][1]))
                    em.wait("sync", (sP.h, epi_done[n][1]))
                    em.add("sync", lambda e, n=n, par=par:
                           e.dma_start(out=out[n * 256:(n + 1) * 256, :].rearrange("(a p) e -> p a e", p=128),
                                       in_=o_stage[par][:]).then_inc(sOUT[par].h, 16))
                    out_cnt[par] += 1
                    sOUT[par].inc(16)
                    if n + 2 < NW:
                        prefetch(n + 2)

                def emit_window_tail(n):
                    tail_epi[n] = ([], [])
                    for sgrp in range(4):
                        tail_jobs.append(lambda n=n, sgrp=sgrp: tail_group(n, sgrp))
                    tail_jobs.append(lambda n=n: tail_finish(n))

                for slot in range(NHH + 3):
                    if 2 <= slot <= NHH + 1:
                        emit_V(slot - 2)
                    if slot >= 3:
                        emit_R(slot - 3)
                    if slot < NHH:
                        emit_F(slot)
                    if slot >= 3:
                        nb, hb = divmod(slot - 3, H)
                        if hb == H - 1:
                            emit_window_tail(nb)
                    if tail_jobs:
                        tail_jobs.pop(0)()
                while tail_jobs:
                    tail_jobs.pop(0)()

                em.wait("sync", (sOUT[0].h, sOUT[0].n))
                em.wait("sync", (sOUT[1].h, sOUT[1].n))

                with nc.Block("attn") as blk:
                    em.run(blk)

    return nc


_BUILD_CACHE = {}


def _get_nc():
    if "nc" not in _BUILD_CACHE:
        _BUILD_CACHE["nc"] = build_nc()
    return _BUILD_CACHE["nc"]


def make_in_maps(x, wq1, wq2, wk1, wk2, wv_, wout_, bout_, lvec, wg, wu, wd):
    iden = np.eye(128, dtype=np.float32)
    in_maps = []
    for core in range(NCORE):
        b, qq = core // 4, core % 4
        cos, sin = _core_tables(qq)
        in_maps.append(dict(
            x_attn=np.ascontiguousarray(x[b, 2048 * qq:2048 * qq + AW, :]),
            x_ffn=np.ascontiguousarray(x[b, 4096 * qq:4096 * (qq + 1), :]),
            wq1=wq1, wq2=wq2, wk1=wk1, wk2=wk2, wv=wv_, wout=wout_, bout=bout_,
            lvec=lvec, wg_i=wg, wu_i=wu, wd_i=wd,
            tcos=cos, tsin=sin, paT=_PAT, pbT=_PBT, iden=iden,
        ))
    return in_maps


def kernel(L0_attn_x, L0_attn_Wq1, L0_attn_Wq2, L0_attn_Wk1, L0_attn_Wk2,
           L0_attn_Wv, L0_attn_Wout, L0_attn_bout, L0_attn_lq1, L0_attn_lk1,
           L0_attn_lq2, L0_attn_lk2, L0_ffn_w_gate, L0_ffn_w_up, L0_ffn_w_down):
    x = np.ascontiguousarray(np.asarray(L0_attn_x, dtype=np.float32))
    f32 = lambda a: np.ascontiguousarray(np.asarray(a, dtype=np.float32))
    i32 = lambda a: np.ascontiguousarray(np.asarray(a, dtype=np.int32))
    lvec = np.concatenate([f32(L0_attn_lq1).ravel(), f32(L0_attn_lk1).ravel(),
                           f32(L0_attn_lq2).ravel(), f32(L0_attn_lk2).ravel()]).reshape(1, 256)
    nc = _get_nc()
    in_maps = make_in_maps(x, f32(L0_attn_Wq1), f32(L0_attn_Wq2), f32(L0_attn_Wk1),
                           f32(L0_attn_Wk2), f32(L0_attn_Wv), f32(L0_attn_Wout),
                           f32(L0_attn_bout).reshape(1, E), lvec,
                           i32(L0_ffn_w_gate), i32(L0_ffn_w_up), i32(L0_ffn_w_down))
    res = run_bass_kernel_spmd(nc, in_maps, core_ids=list(range(NCORE)))
    outp = np.empty((B, L, E), dtype=np.float32)
    for core in range(NCORE):
        b, qq = core // 4, core % 4
        outp[b, 4096 * qq:4096 * (qq + 1), :] = res.results[core]["out"]
    return outp


# version 26 (final)
# speedup vs baseline: 1.1627x; 1.1627x over previous
import numpy as np
from contextlib import ExitStack

import concourse.bass as bass
import concourse.mybir as mybir
from concourse.bass_utils import run_bass_kernel_spmd

F32 = mybir.dt.float32
BF16 = mybir.dt.bfloat16
I32 = mybir.dt.int32
AF = mybir.ActivationFunctionType
ALU = mybir.AluOpType
AX = mybir.AxisListType

B, L, E, H, G, W, D = 2, 16384, 1024, 16, 2, 256, 64
HID = 3072
NCORE = 8
QPC = 4096
NW = 16
AW = 2176
ACH = AW // 128
PCH = QPC // 128
NCH = PCH + ACH
TGRP = 512
NGRP = QPC // TGRP
CPG = TGRP // 128
U_SHIFT = 12.0
EPS = 1e-5
INV127 = 1.0 / 127.0
INV127SQ = 1.0 / (127.0 * 127.0)
TT_SPLITS = [(0, 512), (512, 512), (1024, 512), (1536, 512), (2048, 128)]


class Sem:
    def __init__(self, h):
        self.h = h
        self.n = 0

    def inc(self, k=1):
        self.n += k
        return self.n


class Em:
    ENGS = ("sync", "gpsimd", "tensor", "vector", "scalar")

    def __init__(self):
        self.q = {e: [] for e in self.ENGS}

    def add(self, eng, fn):
        self.q[eng].append(fn)

    def wait(self, eng, tok):
        if tok is not None:
            self.add(eng, lambda e, s=tok[0], v=tok[1]: e.wait_ge(s, v))

    def run(self, blk):
        for eng in self.ENGS:
            fns = self.q[eng]
            getattr(blk, eng)(lambda e, fns=fns: [f(e) for f in fns])


def _rope_tables():
    dpos = np.arange(0, D, 2, dtype=np.float64)
    inv = np.exp(dpos * (-np.log(10000.0) / D))
    pa = np.zeros((64, 64), np.float32)
    pb = np.zeros((64, 64), np.float32)
    for dp in range(32):
        pa[dp, 2 * dp] = 1.0
        pa[32 + dp, 2 * dp + 1] = 1.0
        pb[dp, 2 * dp + 1] = -1.0
        pb[32 + dp, 2 * dp] = 1.0
    z = np.zeros_like(pa)
    pa128 = np.block([[pa, z], [z, pa]])
    pb128 = np.block([[pb, z], [z, pb]])
    paT = np.ascontiguousarray(pa128.T)
    pbT = np.ascontiguousarray(pb128.T)
    freqs128 = np.concatenate([inv, inv, inv, inv])
    return paT, pbT, freqs128


_PAT, _PBT, _FREQS128 = _rope_tables()


def _core_tables(qq):
    a0 = 2048 * qq
    pos = (a0 + np.arange(AW, dtype=np.float64))[None, :]
    ang = _FREQS128[:, None] * pos
    return np.cos(ang).astype(np.float32), np.sin(ang).astype(np.float32)


def build_nc(stages=3):
    nc = bass.Bass("TRN2", target_bir_lowering=False)
    ap = lambda name, shape, dt, kind: nc.dram_tensor(name, shape, dt, kind=kind).ap()

    x_attn = ap("x_attn", [AW, E], F32, "ExternalInput")
    x_ffn = ap("x_ffn", [QPC, E], F32, "ExternalInput")
    wq1 = ap("wq1", [E, E], F32, "ExternalInput")
    wq2 = ap("wq2", [E, E], F32, "ExternalInput")
    wk1 = ap("wk1", [E, 128], F32, "ExternalInput")
    wk2 = ap("wk2", [E, 128], F32, "ExternalInput")
    wv = ap("wv", [E, 128], F32, "ExternalInput")
    wout = ap("wout", [E, E], F32, "ExternalInput")
    bout = ap("bout", [1, E], F32, "ExternalInput")
    lvec = ap("lvec", [1, 256], F32, "ExternalInput")
    wg_i = ap("wg_i", [HID, E], I32, "ExternalInput")
    wu_i = ap("wu_i", [HID, E], I32, "ExternalInput")
    wd_i = ap("wd_i", [E, HID], I32, "ExternalInput")
    tcos = ap("tcos", [128, AW], F32, "ExternalInput")
    tsin = ap("tsin", [128, AW], F32, "ExternalInput")
    paT = ap("paT", [128, 128], F32, "ExternalInput")
    pbT = ap("pbT", [128, 128], F32, "ExternalInput")
    iden = ap("iden", [128, 128], F32, "ExternalInput")

    out = ap("out", [QPC, E], F32, "ExternalOutput")

    scr_gu = ap("scr_gu", [24 * 128, 2048], BF16, "Internal")
    scr_g = ap("scr_g", [QPC, E], F32, "Internal")

    with ExitStack() as ctx:
        sb = lambda n, s, d: ctx.enter_context(nc.sbuf_tensor(n, s, d))
        mksem = lambda n: Sem(ctx.enter_context(nc.semaphore(n)))

        ab_scale = sb("ab_scale", [128, NCH], F32)
        ab_bias = sb("ab_bias", [128, NCH], F32)
        ident_sb = sb("ident_sb", [128, 128], BF16)
        ubias = sb("ubias", [128, 1], F32)
        zbias = sb("zbias", [128, 1], F32)
        epsb = sb("epsb", [128, 1], F32)
        ones1 = sb("ones1", [1, 128], F32)

        sT = mksem("sT")
        sV = mksem("sV")
        sC = mksem("sC")
        sP = mksem("sP")
        sK0 = mksem("sK0")
        sWPL = [mksem(f"sWPL{i}") for i in range(3)]
        sWPS = [mksem(f"sWPS{i}") for i in range(2)]
        sFX = [mksem(f"sFX{i}") for i in range(2)]
        sWT = [mksem(f"sWT{i}") for i in range(2)]
        sGW = mksem("sGW")
        sK2 = mksem("sK2")
        sK2S = mksem("sK2S")
        sXA = [mksem(f"sXA{i}") for i in range(4)]
        sGR = [mksem(f"sGR{i}") for i in range(2)]
        sXW = [mksem(f"sXW{i}") for i in range(2)]
        sOUT = [mksem(f"sOUT{i}") for i in range(2)]

        c01 = ctx.enter_context(ExitStack())
        wdT_sb = c01.enter_context(nc.sbuf_tensor("wdT_sb", [128, 24 * E], BF16))
        wdT_sb3 = wdT_sb[:].rearrange("p (a b) -> p a b", b=E)
        with ExitStack() as c0:
            sb0 = lambda n, s, d: c0.enter_context(nc.sbuf_tensor(n, s, d))
            ps0 = lambda n, s, d: c0.enter_context(nc.psum_tensor(n, s, d))
            wrow = [sb0(f"wrow{i}", [128, HID], BF16) for i in range(3)]
            stag = [sb0(f"stag{i}", [128, 8, 128], BF16) for i in range(2)]
            pstr = [ps0(f"pstr{i}", [128, 128], BF16) for i in range(6)]

            em = Em()

            em.add("vector", lambda e: e.memset(ubias[:], -U_SHIFT))
            em.add("vector", lambda e: e.memset(zbias[:], 0.0))
            em.add("vector", lambda e: e.memset(epsb[:], EPS))
            em.add("vector", lambda e: e.memset(ones1[:], 1.0).then_inc(sV.h, 1))
            consts_done = sV.inc()
            em.add("gpsimd", lambda e: e.dma_start(out=ident_sb[:], in_=iden).then_inc(sK0.h, 16))
            ident_ld = sK0.inc(16)

            prep_trel = {}
            prep_srel = {}
            trevac = {}
            evn = [0]
            em.wait("scalar", (sV.h, consts_done))

            def prep_matrix(src, n_m, base_pi, store_cb=None, dst_cb=None):
                for m in range(n_m):
                    pi = base_pi + m
                    buf = wrow[pi % 3]
                    stg = stag[pi % 2]
                    wbytes = src.shape[1]
                    if pi >= 3:
                        em.wait("gpsimd", (sT.h, prep_trel[pi - 3]))
                    em.add("gpsimd", lambda e, b=buf, s=src, m=m, w=wbytes, q=pi % 3:
                           e.dma_start(out=b[:, 0:w], in_=s[m * 128:(m + 1) * 128, :])
                           .then_inc(sWPL[q].h, 16))
                    ld = (sWPL[pi % 3].h, sWPL[pi % 3].inc(16))
                    em.wait("tensor", ld)
                    if pi == 0:
                        em.wait("tensor", (sK0.h, ident_ld))
                    nk = src.shape[1] // 128
                    war_stg = prep_srel.get(pi - 2) if store_cb is not None else None
                    for k in range(nk):
                        bank = pstr[evn[0] % 6]
                        if evn[0] >= 6:
                            em.wait("tensor", trevac[evn[0] - 6])
                        em.add("tensor", lambda e, b=bank, w=buf, k=k:
                               e.transpose(b[:], w[:, k * 128:(k + 1) * 128],
                                           ident_sb[:]).then_inc(sT.h, 1))
                        tc = sT.inc()
                        dst = stg[:, k, :] if store_cb is not None else dst_cb(m, k)
                        if evn[0] % 2 == 0:
                            em.wait("vector", (sT.h, tc))
                            if k <= 1 and war_stg is not None:
                                em.wait("vector", war_stg)
                            em.add("vector", lambda e, b=bank, d=dst:
                                   e.tensor_copy(d, b[:]).then_inc(sV.h, 1))
                            trevac[evn[0]] = (sV.h, sV.inc())
                        else:
                            em.wait("scalar", (sT.h, tc))
                            if k <= 1 and war_stg is not None:
                                em.wait("scalar", war_stg)
                            em.add("scalar", lambda e, b=bank, d=dst:
                                   e.activation(d, b[:], AF.Identity, bias=zbias[:],
                                                scale=1.0).then_inc(sC.h, 1))
                            trevac[evn[0]] = (sC.h, sC.inc())
                        evn[0] += 1
                    prep_trel[pi] = sT.n
                    if store_cb is not None:
                        em.wait("sync", trevac[evn[0] - 1])
                        em.wait("sync", trevac[evn[0] - 2])
                        em.add("sync", lambda e, s=stg, m=m, cb=store_cb, q=pi % 2:
                               e.dma_start(out=cb(m), in_=s[:]).then_inc(sWPS[q].h, 16))
                        prep_srel[pi] = (sWPS[pi % 2].h, sWPS[pi % 2].inc(16))

            prep_matrix(wg_i, 24, 0,
                        store_cb=lambda m: scr_gu[m * 128:(m + 1) * 128, 0:1024])
            prep_matrix(wu_i, 24, 24,
                        store_cb=lambda m: scr_gu[m * 128:(m + 1) * 128, 1024:2048])
            prep_matrix(wd_i, 8, 48,
                        dst_cb=lambda m, k: wdT_sb3[:, k, m * 128:(m + 1) * 128])

            b0_wps = (sWPS[0].n, sWPS[1].n)
            with nc.Block("prep") as blk:
                em.run(blk)

        if stages >= 2:
          with ExitStack() as c1:
            sb1 = lambda n, s, d: c1.enter_context(nc.sbuf_tensor(n, s, d))
            ps1 = lambda n, s, d: c1.enter_context(nc.psum_tensor(n, s, d))
            xc1 = [sb1(f"xc1_{i}", [128, E], F32) for i in range(2)]
            bnb = sb1("bnb", [128, 2, 12], F32)
            mvb = sb1("mvb", [128, 2, 2], F32)
            lnc = sb1("lnc", [128, 2], F32)
            tmpc = sb1("tmpc", [128, 2], F32)
            xn_tm = sb1("xn_tm", [128, CPG, E], BF16)
            xn_fm = [sb1(f"xn_fm{i}", [128, 8, TGRP], BF16) for i in range(2)]
            wgut = [sb1(f"wgut{i}", [128, 16, 128], BF16) for i in range(2)]
            silu_g = [sb1(f"silu_g{i}", [128, TGRP], BF16) for i in range(2)]
            h_sb = sb1("h_sb", [128, 24, TGRP], BF16)
            g_stage = [sb1(f"g_stage{i}", [128, E], F32) for i in range(2)]
            ps_xt = [ps1(f"ps_xt{i}", [128, 128], BF16) for i in range(2)]
            ps_gu = [ps1(f"ps_gu{i}", [128, TGRP], F32) for i in range(4)]
            ps_dn = [ps1(f"ps_dn{i}", [128, TGRP], F32) for i in range(2)]

            em = Em()
            em.wait("sync", (sWPS[0].h, b0_wps[0]))
            em.wait("sync", (sWPS[1].h, b0_wps[1]))
            em.wait("sync", (sWPL[0].h, sWPL[0].n))
            em.wait("sync", (sWPL[1].h, sWPL[1].n))
            em.wait("sync", (sWPL[2].h, sWPL[2].n))
            em.wait("sync", (sK0.h, sK0.n))

            xe_done = {}
            ln1_done = {}

            def chunk_stats(ch, par, buf, ld):
                em.wait("vector", ld)
                if ch - 2 in ln1_done:
                    em.wait("vector", (sC.h, ln1_done[ch - 2]))
                em.add("vector", lambda e, b=buf, par=par:
                       e.bn_stats(bnb[:, par, 0:6], b[:, 0:512]))
                em.add("vector", lambda e, b=buf, par=par:
                       e.bn_stats(bnb[:, par, 6:12], b[:, 512:1024]).then_inc(sV.h, 1))
                em.wait("vector", (sV.h, sV.inc()))
                em.add("vector", lambda e, par=par:
                       e.bn_aggr(mvb[:, par, :], bnb[:, par, :]).then_inc(sV.h, 1))
                aggr = sV.inc()
                em.wait("scalar", (sV.h, aggr))
                em.add("scalar", lambda e, par=par:
                       e.activation(lnc[:, par:par + 1], mvb[:, par, 1:2], AF.Ln,
                                    bias=epsb[:], scale=1.0).then_inc(sC.h, 1))
                ln1_done[ch] = sC.inc()
                em.wait("scalar", (sC.h, ln1_done[ch]))
                em.add("scalar", lambda e, ch=ch, par=par:
                       e.activation(ab_scale[:, ch:ch + 1], lnc[:, par:par + 1], AF.Exp,
                                    bias=zbias[:], scale=-0.5).then_inc(sC.h, 1))
                rstd = sC.inc()
                em.wait("vector", (sC.h, rstd))
                em.add("vector", lambda e, ch=ch, par=par:
                       e.tensor_tensor(tmpc[:, par:par + 1], mvb[:, par, 0:1],
                                       ab_scale[:, ch:ch + 1], ALU.mult).then_inc(sV.h, 1))
                em.wait("vector", (sV.h, sV.inc()))
                em.add("vector", lambda e, ch=ch, par=par:
                       e.tensor_scalar(ab_bias[:, ch:ch + 1], tmpc[:, par:par + 1],
                                       -1.0, None, ALU.mult).then_inc(sV.h, 1))
                return sV.inc()
            mm_done = {}
            sg_done = {}
            h_done = {}
            dn_done = {}
            g_done = {}
            wld = {}
            wrel = {}
            xtrel = {}
            xt_last = {}
            gt = [0]

            def emit_xn_chunk(grp, cc):
                ch = grp * CPG + cc
                par = ch % 2
                buf = xc1[par]
                if ch >= 2:
                    em.wait("sync", (sC.h, xe_done[ch - 2]))
                em.add("sync", lambda e, b=buf, ch=ch, par=par:
                       e.dma_start(out=b[:], in_=x_ffn[ch * 128:(ch + 1) * 128, :])
                       .then_inc(sFX[par].h, 16))
                ld = (sFX[par].h, sFX[par].inc(16))
                bias_ok = chunk_stats(ch, par, buf, ld)
                em.wait("scalar", ld)
                em.wait("scalar", (sV.h, bias_ok))
                if ch >= 4:
                    em.wait("scalar", (sV.h, xtrel[(ch - 4) * 8 + 7]))
                em.add("scalar", lambda e, b=buf, cc=cc, ch=ch:
                       e.activation(xn_tm[:, cc, :], b[:], AF.Identity,
                                    bias=ab_bias[:, ch:ch + 1],
                                    scale=ab_scale[:, ch:ch + 1]).then_inc(sC.h, 1))
                xe_done[ch] = sC.inc()

            def emit_transpose_pair(grp, t2):
                fm = xn_fm[grp % 2]
                for t in (t2, t2 + 1):
                    cc, k = divmod(t, 8)
                    g = gt[0]
                    j = g % 2
                    if t % 8 == 0:
                        em.wait("tensor", (sC.h, xe_done[grp * CPG + cc]))
                    if g >= 2:
                        em.wait("tensor", (sV.h, xtrel[g - 2]))
                    em.add("tensor", lambda e, cc=cc, k=k, j=j:
                           e.transpose(ps_xt[j][:], xn_tm[:, cc, k * 128:(k + 1) * 128],
                                       ident_sb[:]).then_inc(sT.h, 1))
                    tc = sT.inc()
                    em.wait("vector", (sT.h, tc))
                    em.add("vector", lambda e, fm=fm, cc=cc, k=k, j=j:
                           e.tensor_copy(fm[:, k, cc * 128:(cc + 1) * 128],
                                         ps_xt[j][:]).then_inc(sV.h, 1))
                    xtrel[g] = sV.inc()
                    gt[0] += 1
                if t2 + 2 == 32:
                    xt_last[grp] = sV.n

            for cc in range(CPG):
                emit_xn_chunk(0, cc)
            for t2 in range(0, 32, 2):
                emit_transpose_pair(0, t2)

            for grp in range(NGRP):
                fm = xn_fm[grp % 2]
                for m in range(24):
                    if grp + 1 < NGRP:
                        if m % 2 == 0 and m < 8:
                            emit_xn_chunk(grp + 1, m // 2)
                        if m >= 8:
                            emit_transpose_pair(grp + 1, (m - 8) * 2)
                    par = m % 2
                    pkey = (grp, m - 2) if m >= 2 else ((grp - 1, 22 + m) if grp > 0 else None)
                    if pkey is not None:
                        em.wait("sync", (sT.h, wrel[pkey]))
                    em.add("sync", lambda e, m=m, par=par:
                           e.dma_start(out=wgut[par][:],
                                       in_=scr_gu[m * 128:(m + 1) * 128, :])
                           .then_inc(sWT[par].h, 16))
                    wld[(grp, m)] = (sWT[par].h, sWT[par].inc(16))
                    em.wait("tensor", wld[(grp, m)])
                    if m == 0:
                        em.wait("tensor", (sV.h, xt_last[grp]))
                    if pkey is not None:
                        em.wait("tensor", (sC.h, sg_done[pkey]))
                        em.wait("tensor", (sV.h, h_done[pkey]))
                    if m <= 1 and grp > 0:
                        em.wait("tensor", (sC.h, g_done[(grp - 1, 3)]))
                    wt = wgut[par]
                    for k in range(8):
                        em.add("tensor", lambda e, wt=wt, k=k, par=par, fm=fm:
                               e.matmul(ps_gu[par][:], wt[:, k, :], fm[:, k, :],
                                        start=(k == 0), stop=(k == 7)))
                    for k in range(8):
                        last = k == 7
                        em.add("tensor", lambda e, wt=wt, k=k, par=par, fm=fm, last=last:
                               (e.matmul(ps_gu[2 + par][:], wt[:, 8 + k, :], fm[:, k, :],
                                         start=(k == 0), stop=last).then_inc(sT.h, 1)
                                if last else
                                e.matmul(ps_gu[2 + par][:], wt[:, 8 + k, :], fm[:, k, :],
                                         start=(k == 0), stop=last)))
                    mm_done[(grp, m)] = sT.inc()
                    wrel[(grp, m)] = mm_done[(grp, m)]
                    em.wait("scalar", (sT.h, mm_done[(grp, m)]))
                    if pkey is not None:
                        em.wait("scalar", (sV.h, h_done[pkey]))
                    em.add("scalar", lambda e, par=par:
                           e.activation(silu_g[par][:], ps_gu[par][:], AF.Silu,
                                        bias=zbias[:], scale=INV127).then_inc(sC.h, 1))
                    sg_done[(grp, m)] = sC.inc()
                    em.wait("vector", (sC.h, sg_done[(grp, m)]))
                    em.add("vector", lambda e, m=m, par=par:
                           e.tensor_tensor(h_sb[:, m, :], ps_gu[2 + par][:], silu_g[par][:],
                                           ALU.mult).then_inc(sV.h, 1))
                    h_done[(grp, m)] = sV.inc()

                for tt in range(CPG):
                    par = tt % 2
                    pair = (ps_dn[0], ps_dn[1]) if tt % 2 == 0 else (ps_gu[0], ps_gu[1])
                    if tt >= 2:
                        em.wait("tensor", (sC.h, g_done[(grp, tt - 2)]))
                    elif tt == 0:
                        if grp > 0:
                            em.wait("tensor", (sC.h, g_done[(grp - 1, 2)]))
                        em.wait("tensor", (sV.h, h_done[(grp, 23)]))
                    else:
                        em.wait("tensor", (sC.h, sg_done[(grp, 23)]))
                    for half in range(2):
                        for k in range(24):
                            last = (k == 23 and half == 1)
                            em.add("tensor", lambda e, tt=tt, k=k, half=half,
                                   pair=pair, last=last:
                                   (e.matmul(pair[half][:], h_sb[:, k, tt * 128:(tt + 1) * 128],
                                             wdT_sb3[:, k, half * 512:(half + 1) * 512],
                                             start=(k == 0), stop=(k == 23)).then_inc(sT.h, 1)
                                    if last else
                                    e.matmul(pair[half][:], h_sb[:, k, tt * 128:(tt + 1) * 128],
                                             wdT_sb3[:, k, half * 512:(half + 1) * 512],
                                             start=(k == 0), stop=(k == 23))))
                    dn_done[(grp, tt)] = sT.inc()
                    em.wait("scalar", (sT.h, dn_done[(grp, tt)]))
                    if sGW.n >= 32:
                        em.wait("scalar", (sGW.h, sGW.n))
                    for half in range(2):
                        last = half == 1
                        em.add("scalar", lambda e, half=half, par=par, pair=pair, last=last:
                               e.activation(g_stage[par][:, half * 512:(half + 1) * 512],
                                            pair[half][:], AF.Silu,
                                            bias=zbias[:], scale=INV127SQ).then_inc(sC.h, 1))
                        sC.inc()
                    g_done[(grp, tt)] = sC.n
                    em.wait("scalar", (sC.h, g_done[(grp, tt)]))
                    r0 = (grp * CPG + tt) * 128
                    em.add("scalar", lambda e, r0=r0, par=par:
                           e.dma_start(out=scr_g[r0:r0 + 128, :],
                                       in_=g_stage[par][:]).then_inc(sGW.h, 16))
                    sGW.inc(16)

            b1_gw = sGW.n
            with nc.Block("ffn") as blk:
                em.run(blk)
        c01.close()

        if stages >= 3:
          with ExitStack() as c2:
            sb2 = lambda n, s, d: c2.enter_context(nc.sbuf_tensor(n, s, d))
            q1r = sb2("q1r", [128, 8, AW], BF16)
            q2r = sb2("q2r", [128, 8, AW], BF16)
            k1r = sb2("k1r", [128, AW], BF16)
            k2r = sb2("k2r", [128, AW], BF16)
            v0_tm = sb2("v0_tm", [128, ACH, 128], BF16)
            v1_tm = sb2("v1_tm", [128, ACH, 128], BF16)
            lamt = sb2("lamt", [128, 8], F32)

            em = Em()
            for eng in ("sync", "gpsimd"):
                em.wait(eng, (sGW.h, b1_gw))
                em.wait(eng, (sWT[0].h, sWT[0].n))
                em.wait(eng, (sWT[1].h, sWT[1].n))
                em.wait(eng, (sFX[0].h, sFX[0].n))
                em.wait(eng, (sFX[1].h, sFX[1].n))

            em.add("vector", lambda e: e.memset(v0_tm[:], 0.0))
            em.add("vector", lambda e: e.memset(v1_tm[:], 0.0).then_inc(sV.h, 1))
            em.wait("vector", (sV.h, sV.inc()))
            em.add("vector", lambda e: e.memset(v0_tm[:, :, 64:65], 1.0))
            em.add("vector", lambda e: e.memset(v1_tm[:, :, 0:1], 1.0).then_inc(sV.h, 1))
            vinit_done = sV.inc()

            with ExitStack() as cp:
                sbp = lambda n, s, d: cp.enter_context(nc.sbuf_tensor(n, s, d))
                psp = lambda n, s, d: cp.enter_context(nc.psum_tensor(n, s, d))
                wq1_sb = sbp("wq1_sb", [128, 8, E], BF16)
                wq2_sb = sbp("wq2_sb", [128, 8, E], BF16)
                wk1_sb = sbp("wk1_sb", [128, 8, 128], BF16)
                wk2_sb = sbp("wk2_sb", [128, 8, 128], BF16)
                wv_sb = sbp("wv_sb", [128, 8, 128], BF16)
                paT_sb = sbp("paT_sb", [128, 128], BF16)
                pbT_sb = sbp("pbT_sb", [128, 128], BF16)
                cos_sb = sbp("cos_sb", [128, AW], F32)
                sin_sb = sbp("sin_sb", [128, AW], F32)
                xna_fm = sbp("xna_fm", [128, 8, AW], BF16)
                qf = sbp("qf", [128, 2, 512], BF16)
                rt1 = sbp("rt1", [128, 2, 512], BF16)
                rt2 = sbp("rt2", [128, 2, 512], BF16)
                lv_sb = sbp("lv_sb", [1, 256], F32)
                dots = sbp("dots", [1, 130], F32)
                ps_p = [psp(f"ps_p{i}", [128, 512], F32) for i in range(2)]
                ps_ra = psp("ps_ra", [128, 512], F32)
                ps_rb = psp("ps_rb", [128, 512], F32)
                ps_vt = [psp(f"ps_vt{i}", [128, 128], BF16) for i in range(2)]
                ps_lam = psp("ps_lam", [128, 2], F32)

                for t_ap, t_sb in [(wq1, wq1_sb), (wq2, wq2_sb), (wk1, wk1_sb),
                                   (wk2, wk2_sb), (wv, wv_sb)]:
                    em.add("gpsimd", lambda e, a=t_ap, s=t_sb:
                           e.dma_start(out=s[:], in_=a.rearrange("(k p) n -> p k n", p=128))
                           .then_inc(sK2.h, 16))
                    sK2.inc(16)
                em.add("gpsimd", lambda e: e.dma_start(out=paT_sb[:], in_=paT).then_inc(sK2.h, 16))
                em.add("gpsimd", lambda e: e.dma_start(out=pbT_sb[:], in_=pbT).then_inc(sK2.h, 16))
                em.add("gpsimd", lambda e: e.dma_start(out=cos_sb[:], in_=tcos).then_inc(sK2.h, 16))
                em.add("gpsimd", lambda e: e.dma_start(out=sin_sb[:], in_=tsin).then_inc(sK2.h, 16))
                sK2.inc(16 * 4)
                em.add("sync", lambda e: e.dma_start(out=lv_sb[:], in_=lvec).then_inc(sK2S.h, 16))
                lv_ld = sK2S.inc(16)
                wload = sK2.n

                em.wait("vector", (sK2S.h, lv_ld))
                em.add("vector", lambda e: e.tensor_tensor(dots[:, 0:64], lv_sb[:, 0:64],
                                                           lv_sb[:, 64:128], ALU.mult))
                em.add("vector", lambda e: e.tensor_tensor(dots[:, 64:128], lv_sb[:, 128:192],
                                                           lv_sb[:, 192:256], ALU.mult).then_inc(sV.h, 1))
                dmul = sV.inc()
                em.wait("vector", (sV.h, dmul))
                em.add("vector", lambda e: e.tensor_reduce(dots[:, 128:129], dots[:, 0:64], AX.X, ALU.add))
                em.add("vector", lambda e: e.tensor_reduce(dots[:, 129:130], dots[:, 64:128],
                                                           AX.X, ALU.add).then_inc(sV.h, 1))
                dred = sV.inc()
                em.wait("tensor", (sV.h, dred))
                em.add("tensor", lambda e: e.matmul(ps_lam[:], ones1[:], dots[:, 128:130],
                                                    start=True, stop=True).then_inc(sT.h, 1))
                dbc = sT.inc()
                em.wait("scalar", (sT.h, dbc))
                em.add("scalar", lambda e: e.activation(lamt[:, 2:4], ps_lam[:], AF.Exp,
                                                        bias=zbias[:], scale=1.0).then_inc(sC.h, 1))
                lexp = sC.inc()
                em.wait("vector", (sC.h, lexp))
                em.add("vector", lambda e: e.tensor_tensor(lamt[:, 4:5], lamt[:, 2:3],
                                                           lamt[:, 3:4], ALU.subtract).then_inc(sV.h, 1))
                em.wait("vector", (sV.h, sV.inc()))
                em.add("vector", lambda e: e.tensor_scalar(lamt[:, 5:6], lamt[:, 4:5], 0.8, None,
                                                           ALU.add).then_inc(sV.h, 1))
                em.wait("vector", (sV.h, sV.inc()))
                em.add("vector", lambda e: e.tensor_scalar(lamt[:, 6:7], lamt[:, 5:6], 0.1, 0.9,
                                                           ALU.max, ALU.min).then_inc(sV.h, 1))
                em.wait("vector", (sV.h, sV.inc()))
                em.add("vector", lambda e: e.tensor_scalar(lamt[:, 1:2], lamt[:, 6:7], -1.0, None,
                                                           ALU.mult).then_inc(sV.h, 1))
                lam_done = sV.inc()

                with ExitStack() as cx:
                    xa_tm = cx.enter_context(nc.sbuf_tensor("xa_tm", [128, 4, E], BF16))
                    xa_f32 = [cx.enter_context(nc.sbuf_tensor(f"xa_f32_{i}", [128, E], F32))
                              for i in range(4)]
                    bnb2 = cx.enter_context(nc.sbuf_tensor("bnb2", [128, 2, 12], F32))
                    mvb2 = cx.enter_context(nc.sbuf_tensor("mvb2", [128, 2, 2], F32))
                    lnc2 = cx.enter_context(nc.sbuf_tensor("lnc2", [128, 2], F32))
                    tmpc2 = cx.enter_context(nc.sbuf_tensor("tmpc2", [128, 2], F32))
                    ln2_done = {}

                    def xa_stats(ch, par, buf, ld):
                        em.wait("vector", ld)
                        if ch - 2 in ln2_done:
                            em.wait("vector", (sC.h, ln2_done[ch - 2]))
                        em.add("vector", lambda e, b=buf, par=par:
                               e.bn_stats(bnb2[:, par, 0:6], b[:, 0:512]))
                        em.add("vector", lambda e, b=buf, par=par:
                               e.bn_stats(bnb2[:, par, 6:12], b[:, 512:1024]).then_inc(sV.h, 1))
                        em.wait("vector", (sV.h, sV.inc()))
                        em.add("vector", lambda e, par=par:
                               e.bn_aggr(mvb2[:, par, :], bnb2[:, par, :]).then_inc(sV.h, 1))
                        aggr = sV.inc()
                        em.wait("scalar", (sV.h, aggr))
                        em.add("scalar", lambda e, par=par:
                               e.activation(lnc2[:, par:par + 1], mvb2[:, par, 1:2], AF.Ln,
                                            bias=epsb[:], scale=1.0).then_inc(sC.h, 1))
                        ln2_done[ch] = sC.inc()
                        em.wait("scalar", (sC.h, ln2_done[ch]))
                        em.add("scalar", lambda e, ch=ch, par=par:
                               e.activation(ab_scale[:, ch:ch + 1], lnc2[:, par:par + 1],
                                            AF.Exp, bias=zbias[:], scale=-0.5).then_inc(sC.h, 1))
                        rstd = sC.inc()
                        em.wait("vector", (sC.h, rstd))
                        em.add("vector", lambda e, ch=ch, par=par:
                               e.tensor_tensor(tmpc2[:, par:par + 1], mvb2[:, par, 0:1],
                                               ab_scale[:, ch:ch + 1], ALU.mult).then_inc(sV.h, 1))
                        em.wait("vector", (sV.h, sV.inc()))
                        em.add("vector", lambda e, ch=ch, par=par:
                               e.tensor_scalar(ab_bias[:, ch:ch + 1], tmpc2[:, par:par + 1],
                                               -1.0, None, ALU.mult).then_inc(sV.h, 1))
                        return sV.inc()

                    xa_done = {}
                    xat_done = {}
                    xarel = {}
                    for i in range(ACH):
                        par = i % 4
                        buf = xa_f32[par]
                        if i >= 4:
                            em.wait("sync", (sC.h, xa_done[i - 4]))
                        em.add("sync", lambda e, b=buf, i=i, par=par:
                               e.dma_start(out=b[:], in_=x_attn[i * 128:(i + 1) * 128, :])
                               .then_inc(sXA[par].h, 16))
                        ld = (sXA[par].h, sXA[par].inc(16))
                        ch = PCH + i
                        bias_ok = xa_stats(ch, i % 2, buf, ld)
                        em.wait("scalar", ld)
                        em.wait("scalar", (sV.h, bias_ok))
                        if i >= 4:
                            em.wait("scalar", (sT.h, xat_done[i - 4]))
                        em.add("scalar", lambda e, b=buf, i=i, ch=ch:
                               e.activation(xa_tm[:, i % 4, :], b[:], AF.Identity,
                                            bias=ab_bias[:, ch:ch + 1],
                                            scale=ab_scale[:, ch:ch + 1]).then_inc(sC.h, 1))
                        xa_done[i] = sC.inc()
                        em.wait("tensor", (sC.h, xa_done[i]))
                        for k in range(8):
                            t = i * 8 + k
                            j = t % 2
                            if t >= 2:
                                em.wait("tensor", xarel[t - 2])
                            em.add("tensor", lambda e, i=i, k=k, j=j:
                                   e.transpose(ps_vt[j][:], xa_tm[:, i % 4, k * 128:(k + 1) * 128],
                                               ident_sb[:]).then_inc(sT.h, 1))
                            tc = sT.inc()
                            if t % 2 == 0:
                                em.wait("vector", (sT.h, tc))
                                em.add("vector", lambda e, i=i, k=k, j=j:
                                       e.tensor_copy(xna_fm[:, k, i * 128:(i + 1) * 128],
                                                     ps_vt[j][:]).then_inc(sV.h, 1))
                                xarel[t] = (sV.h, sV.inc())
                            else:
                                em.wait("scalar", (sT.h, tc))
                                em.add("scalar", lambda e, i=i, k=k, j=j:
                                       e.activation(xna_fm[:, k, i * 128:(i + 1) * 128],
                                                    ps_vt[j][:], AF.Identity, bias=zbias[:],
                                                    scale=1.0).then_inc(sC.h, 1))
                                xarel[t] = (sC.h, sC.inc())
                        xat_done[i] = sT.n
                    xna_ready = (sV.n, sC.n)

                    proj_i = [0]
                    psem = {}
                    qf_rel = [None]
                    rope_rel = [None]
                    rope_pool = [None]
                    rope_hist = []
                    vt_rel = {0: None, 1: None}

                    def project(w_sb, nm, dest, rope):
                        for m in range(nm):
                            for (tb, tn) in TT_SPLITS:
                                it = proj_i[0]
                                sl = it % 2
                                bank = ps_p[it % 2]
                                if psem.get(it % 2) is not None:
                                    em.wait("tensor", (sC.h, psem[it % 2]))
                                if it == 0:
                                    em.wait("tensor", (sK2.h, wload))
                                    em.wait("tensor", (sV.h, xna_ready[0]))
                                    em.wait("tensor", (sC.h, xna_ready[1]))
                                for k in range(8):
                                    last = k == 7
                                    em.add("tensor", lambda e, b=bank, w=w_sb, m=m, k=k, tb=tb,
                                           tn=tn, last=last:
                                           (e.matmul(b[:, 0:tn], w[:, k, m * 128:(m + 1) * 128],
                                                     xna_fm[:, k, tb:tb + tn],
                                                     start=(k == 0), stop=last).then_inc(sT.h, 1)
                                            if last else
                                            e.matmul(b[:, 0:tn], w[:, k, m * 128:(m + 1) * 128],
                                                     xna_fm[:, k, tb:tb + tn],
                                                     start=(k == 0), stop=last)))
                                mmv = sT.inc()
                                em.wait("scalar", (sT.h, mmv))
                                if qf_rel[0] is not None:
                                    em.wait("scalar", (sT.h, qf_rel[0]))
                                em.add("scalar", lambda e, b=bank, tn=tn, sl=sl:
                                       e.activation(qf[:, sl, 0:tn], b[:, 0:tn], AF.Identity,
                                                    bias=zbias[:], scale=1.0).then_inc(sC.h, 1))
                                psem[it % 2] = sC.inc()
                                if not rope:
                                    em.wait("tensor", (sC.h, psem[it % 2]))
                                    for jj in range(tn // 128):
                                        tch = tb // 128 + jj
                                        vb = tch % 2
                                        if vt_rel[vb] is not None:
                                            em.wait("tensor", (sV.h, vt_rel[vb]))
                                        em.add("tensor", lambda e, jj=jj, sl=sl, vb=vb:
                                               e.transpose(ps_vt[vb][:], qf[:, sl, jj * 128:(jj + 1) * 128],
                                                           ident_sb[:]).then_inc(sT.h, 1))
                                        tc2 = sT.inc()
                                        em.wait("vector", (sT.h, tc2))
                                        if tch == 0:
                                            em.wait("vector", (sV.h, vinit_done))
                                        em.add("vector", lambda e, tch=tch, vb=vb:
                                               e.tensor_copy(v0_tm[:, tch, 0:64], ps_vt[vb][:, 0:64]))
                                        em.add("vector", lambda e, tch=tch, vb=vb:
                                               e.tensor_copy(v1_tm[:, tch, 64:128],
                                                             ps_vt[vb][:, 64:128]).then_inc(sV.h, 1))
                                        vt_rel[vb] = sV.inc()
                                    qf_rel[0] = sT.n
                                else:
                                    em.wait("tensor", (sC.h, psem[it % 2]))
                                    if rope_rel[0] is not None:
                                        em.wait("tensor", (sV.h, rope_rel[0]))
                                    em.add("tensor", lambda e, tn=tn, sl=sl:
                                           e.matmul(ps_ra[:, 0:tn], paT_sb[:], qf[:, sl, 0:tn],
                                                    start=True, stop=True))
                                    em.add("tensor", lambda e, tn=tn, sl=sl:
                                           e.matmul(ps_rb[:, 0:tn], pbT_sb[:], qf[:, sl, 0:tn],
                                                    start=True, stop=True).then_inc(sT.h, 1))
                                    qf_rel[0] = sT.inc()
                                    rp2 = it % 2
                                    em.wait("vector", (sT.h, qf_rel[0]))
                                    if rope_pool[0] is not None and len(rope_hist) >= 2:
                                        em.wait("vector", (sP.h, rope_hist[-2]))
                                    em.add("vector", lambda e, tb=tb, tn=tn, rp2=rp2:
                                           e.tensor_tensor(rt1[:, rp2, 0:tn], ps_ra[:, 0:tn],
                                                           cos_sb[:, tb:tb + tn], ALU.mult))
                                    em.add("vector", lambda e, tb=tb, tn=tn, rp2=rp2:
                                           e.tensor_tensor(rt2[:, rp2, 0:tn], ps_rb[:, 0:tn],
                                                           sin_sb[:, tb:tb + tn], ALU.mult).then_inc(sV.h, 1))
                                    rope_rel[0] = sV.inc()
                                    em.wait("gpsimd", (sV.h, rope_rel[0]))
                                    dst = dest(m, tb, tn)
                                    em.add("gpsimd", lambda e, d=dst, tn=tn, rp2=rp2:
                                           e.tensor_tensor(d, rt1[:, rp2, 0:tn], rt2[:, rp2, 0:tn],
                                                           ALU.add).then_inc(sP.h, 1))
                                    rope_pool[0] = sP.inc()
                                    rope_hist.append(rope_pool[0])
                                proj_i[0] += 1

                    project(wq1_sb, 8, lambda m, tb, tn: q1r[:, m, tb:tb + tn], True)
                    project(wq2_sb, 8, lambda m, tb, tn: q2r[:, m, tb:tb + tn], True)
                    project(wk1_sb, 1, lambda m, tb, tn: k1r[:, tb:tb + tn], True)
                    project(wk2_sb, 1, lambda m, tb, tn: k2r[:, tb:tb + tn], True)
                    project(wv_sb, 1, None, False)
                    em.wait("tensor", (sP.h, sP.n))
                    em.wait("tensor", (sV.h, sV.n))
            proj_end = (sT.n, sV.n, sC.n, sP.n)

            with ExitStack() as cw:
                sbw = lambda n, s, d: cw.enter_context(nc.sbuf_tensor(n, s, d))
                psw = lambda n, s, d: cw.enter_context(nc.psum_tensor(n, s, d))
                wout_sb = sbw("wout_sb", [128, 8, E], BF16)
                bout_sb = sbw("bout_sb", [1, E], F32)
                pe_sb = [sbw(f"pe_sb{p}", [128, 2, 512], BF16) for p in range(2)]
                rz32_sb = sbw("rz32_sb", [128, 2, 512], F32)
                rz_sb = sbw("rz_sb", [128, 2, 512], BF16)
                asb_e = [sbw(f"asb_e{i}", [128, 512], BF16) for i in range(2)]
                tcmb = [sbw(f"tcmb{i}", [128, 512], BF16) for i in range(2)]
                onesq = sbw("onesq", [128, 128], BF16)
                lamt_bf = sbw("lamt_bf", [128, 1], BF16)
                a_sb = [sbw(f"a_sb{i}", [128, 8, 256], BF16) for i in range(2)]
                g_tm = [sbw(f"g_tm{i}", [128, 2, E], F32) for i in range(2)]
                x_win = [sbw(f"x_win{i}", [128, 2, E], F32) for i in range(2)]
                o_stage = [sbw(f"o_stage{i}", [128, 2, E], F32) for i in range(2)]
                prod = sbw("prod", [128, 512], F32)
                ps_s = [[psw(f"ps_s{hp}{p}", [128, 512], F32) for p in range(2)]
                        for hp in range(2)]
                ps_a = psw("ps_a", [128, 512], F32)
                ps_r = psw("ps_r", [128, 2, 512], F32)
                ps_w = psw("ps_w", [128, 512], F32)

                for eng in ("gpsimd", "sync", "scalar"):
                    em.wait(eng, (sT.h, proj_end[0]))
                    em.wait(eng, (sV.h, proj_end[1]))
                    em.wait(eng, (sC.h, proj_end[2]))
                    em.wait(eng, (sP.h, proj_end[3]))
                em.wait("gpsimd", (sK2.h, wload))
                for q in range(4):
                    em.wait("sync", (sXA[q].h, sXA[q].n))
                em.wait("sync", (sK2S.h, lv_ld))

                em.add("gpsimd", lambda e:
                       e.dma_start(out=wout_sb[:], in_=wout.rearrange("(k p) n -> p k n", p=128))
                       .then_inc(sK2.h, 16))
                sK2.inc(16)
                em.add("sync", lambda e: e.dma_start(out=bout_sb[:], in_=bout).then_inc(sK2S.h, 16))
                bout_ld = sK2S.inc(16)
                wb_ld = sK2.n

                em.add("vector", lambda e: e.memset(onesq[:], 1.0).then_inc(sV.h, 1))
                ones_done = sV.inc()
                em.wait("vector", (sV.h, lam_done))
                em.add("vector", lambda e: e.tensor_copy(lamt_bf[:], lamt[:, 1:2]).then_inc(sV.h, 1))
                lambf_done = sV.inc()

                rel_w = [None]
                rel_prod = [None]
                gx_loaded = {}
                epi_done = {}
                out_cnt = [0, 0]

                def prefetch(n):
                    par = n % 2
                    if n == 0:
                        em.wait("scalar", (sGW.h, b1_gw))
                    if n >= 2:
                        em.wait("scalar", (sV.h, epi_done[n - 2][0]))
                        em.wait("sync", (sP.h, epi_done[n - 2][1]))
                    em.add("scalar", lambda e, n=n, par=par:
                           e.dma_start(out=g_tm[par][:],
                                       in_=scr_g[n * 256:(n + 1) * 256, :]
                                       .rearrange("(a p) e -> p a e", p=128))
                           .then_inc(sGR[par].h, 16))
                    sGR[par].inc(16)
                    em.add("sync", lambda e, n=n, par=par:
                           e.dma_start(out=x_win[par][:],
                                       in_=x_ffn[n * 256:(n + 1) * 256, :]
                                       .rearrange("(a p) e -> p a e", p=128)).then_inc(sXW[par].h, 16))
                    sXW[par].inc(16)
                    gx_loaded[n] = ((sGR[par].h, sGR[par].n), (sXW[par].h, sXW[par].n))

                prefetch(0)
                prefetch(1)

                NHH = NW * H
                s1_done = {}
                s2p = {}
                s6_done = {}
                recip_done = {}
                evac_done = {}
                bcast_done = {}
                comb_done = {}

                def emit_F(hh):
                    n, h = divmod(hh, H)
                    hp = hh % 2
                    c, po = h // 2, (h % 2) * 64
                    for p in range(2):
                        qbuf = q1r if p == 0 else q2r
                        kbuf = k1r if p == 0 else k2r
                        if hh >= 2:
                            em.wait("tensor", (sC.h, s2p[(hh - 2, p)]))
                        for kc in range(2):
                            last = kc == 1
                            em.add("tensor", lambda e, q=qbuf, k=kbuf, n=n, kc=kc, po=po,
                                   c=c, p=p, hp=hp, last=last:
                                   (e.matmul(ps_s[hp][p][:, kc * 256:(kc + 1) * 256],
                                             k[po:po + 64, 128 * (n + kc):128 * (n + kc) + 128],
                                             q[po:po + 64, c, 128 * n:128 * n + 256],
                                             start=(kc == 0), stop=last).then_inc(sT.h, 1)
                                    if last else
                                    e.matmul(ps_s[hp][p][:, kc * 256:(kc + 1) * 256],
                                             k[po:po + 64, 128 * (n + kc):128 * (n + kc) + 128],
                                             q[po:po + 64, c, 128 * n:128 * n + 256],
                                             start=(kc == 0), stop=last)))
                        s1_done[(hh, p)] = sT.inc()
                    for p in range(2):
                        em.wait("scalar", (sT.h, s1_done[(hh, p)]))
                        if hh >= 2 and p == 0:
                            em.wait("scalar", (sT.h, s6_done[hh - 2]))
                        em.add("scalar", lambda e, p=p, hp=hp:
                               e.activation(pe_sb[p][:, hp, :], ps_s[hp][p][:], AF.Exp,
                                            bias=ubias[:], scale=0.125).then_inc(sC.h, 1))
                        s2p[(hh, p)] = sC.inc()

                def emit_V(hh):
                    n, h = divmod(hh, H)
                    hp = hh % 2
                    g, po = h % 2, (h % 2) * 64
                    zp = 64 if g == 0 else 0
                    vb = v0_tm if g == 0 else v1_tm
                    mhi = 65 if g == 0 else 128
                    em.wait("tensor", (sC.h, s2p[(hh, 1)]))
                    if hh >= 1:
                        em.wait("tensor", evac_done[hh - 1])
                        em.wait("tensor", recip_done[hh - 1])
                    for p in range(2):
                        for kc in range(2):
                            last = p == 1 and kc == 1
                            em.add("tensor", lambda e, n=n, kc=kc, p=p, hp=hp, vb=vb,
                                   mhi=mhi, last=last:
                                   (e.matmul(ps_a[0:mhi, p * 256:(p + 1) * 256],
                                             vb[:, n + kc, 0:mhi],
                                             pe_sb[p][:, hp, kc * 256:(kc + 1) * 256],
                                             start=(kc == 0), stop=(kc == 1)).then_inc(sT.h, 1)
                                    if last else
                                    e.matmul(ps_a[0:mhi, p * 256:(p + 1) * 256],
                                             vb[:, n + kc, 0:mhi],
                                             pe_sb[p][:, hp, kc * 256:(kc + 1) * 256],
                                             start=(kc == 0), stop=(kc == 1))))
                    s6_done[hh] = sT.inc()
                    rp = (hh // 2) % 2
                    em.wait("scalar", (sT.h, s6_done[hh]))
                    if hh >= 4:
                        em.wait("scalar", (sT.h, bcast_done[hh - 4]))
                    em.add("scalar", lambda e, rp=rp, zp=zp:
                           e.activation(rz32_sb[zp:zp + 1, rp, :], ps_a[zp:zp + 1, :],
                                        AF.Ln, bias=zbias[zp:zp + 1, :],
                                        scale=1.0).then_inc(sC.h, 1))
                    em.wait("scalar", (sC.h, sC.inc()))
                    em.add("scalar", lambda e, rp=rp, zp=zp:
                           e.activation(rz_sb[zp:zp + 1, rp, :], rz32_sb[zp:zp + 1, rp, :],
                                        AF.Exp, bias=zbias[zp:zp + 1, :],
                                        scale=-1.0).then_inc(sC.h, 1))
                    recip_done[hh] = (sC.h, sC.inc())
                    em.wait("vector", (sT.h, s6_done[hh]))
                    if hh >= 2:
                        em.wait("vector", (sV.h, comb_done[hh - 2]))
                    em.add("vector", lambda e, hp=hp, po=po:
                           e.tensor_copy(asb_e[hp][po:po + 64, :],
                                         ps_a[po:po + 64, :]).then_inc(sV.h, 1))
                    evac_done[hh] = (sV.h, sV.inc())

                def emit_R(hh):
                    n, h = divmod(hh, H)
                    hp = hh % 2
                    rp = (hh // 2) % 2
                    c, po = h // 2, (h % 2) * 64
                    zp = 64 if h % 2 == 0 else 0
                    em.wait("tensor", recip_done[hh])
                    if hh == 0:
                        em.wait("tensor", (sV.h, ones_done))
                    if hh >= 2:
                        em.wait("tensor", (sV.h, comb_done[hh - 2]))
                    em.add("tensor", lambda e, hp=hp, rp=rp, zp=zp:
                           e.matmul(ps_r[:, hp, :], onesq[zp:zp + 1, 0:128],
                                    rz_sb[zp:zp + 1, rp, :],
                                    start=True, stop=True).then_inc(sT.h, 1))
                    bcast_done[hh] = sT.inc()
                    em.wait("vector", (sT.h, bcast_done[hh]))
                    em.wait("vector", evac_done[hh])
                    if hh <= 1:
                        em.wait("vector", (sV.h, lambf_done))
                    em.add("vector", lambda e, hp=hp, po=po:
                           e.tensor_tensor(tcmb[hp][po:po + 64, 0:256],
                                           asb_e[hp][po:po + 64, 0:256],
                                           ps_r[po:po + 64, hp, 0:256], ALU.mult))
                    em.add("vector", lambda e, hp=hp, po=po:
                           e.tensor_tensor(tcmb[hp][po:po + 64, 256:512],
                                           asb_e[hp][po:po + 64, 256:512],
                                           ps_r[po:po + 64, hp, 256:512],
                                           ALU.mult).then_inc(sV.h, 1))
                    em.wait("vector", (sV.h, sV.inc()))
                    em.add("vector", lambda e, hp=hp, po=po, c=c, wp=n % 2:
                           e.scalar_tensor_tensor(a_sb[wp][po:po + 64, c, :],
                                                  tcmb[hp][po:po + 64, 256:512],
                                                  lamt_bf[po:po + 64, 0:1],
                                                  tcmb[hp][po:po + 64, 0:256],
                                                  ALU.mult, ALU.add).then_inc(sV.h, 1))
                    comb_done[hh] = sV.inc()

                tail_jobs = []
                tail_epi = {}

                def tail_group(n, sgrp):
                    par = n % 2
                    s, half = [(0, 0), (0, 1), (1, 0), (1, 1)][sgrp]
                    if sgrp == 0:
                        em.wait("tensor", (sV.h, comb_done[n * H + H - 1]))
                        if n == 0:
                            em.wait("tensor", (sK2.h, wb_ld))
                            em.wait("tensor", (sK2S.h, bout_ld))
                    if rel_w[0] is not None:
                        em.wait("tensor", (sV.h, rel_w[0]))
                    em.add("tensor", lambda e, half=half:
                           e.matmul(ps_w[:], ones1[:], bout_sb[:, half * 512:(half + 1) * 512],
                                    start=True, stop=False))
                    for k in range(8):
                        last = k == 7
                        em.add("tensor", lambda e, s=s, half=half, k=k, par=par, last=last:
                               (e.matmul(ps_w[:], a_sb[par][:, k, s * 128:(s + 1) * 128],
                                         wout_sb[:, k, half * 512:(half + 1) * 512],
                                         start=False, stop=last).then_inc(sT.h, 1)
                                if last else
                                e.matmul(ps_w[:], a_sb[par][:, k, s * 128:(s + 1) * 128],
                                         wout_sb[:, k, half * 512:(half + 1) * 512],
                                         start=False, stop=last)))
                    wmm = sT.inc()
                    em.wait("vector", (sT.h, wmm))
                    if sgrp == 0:
                        em.wait("vector", gx_loaded[n][0])
                    if rel_prod[0] is not None:
                        em.wait("vector", (sP.h, rel_prod[0]))
                    em.add("vector", lambda e, s=s, half=half, par=par:
                           e.tensor_tensor(prod[:], ps_w[:],
                                           g_tm[par][:, s, half * 512:(half + 1) * 512],
                                           ALU.mult).then_inc(sV.h, 1))
                    pm = sV.inc()
                    rel_w[0] = pm
                    tail_epi[n][0].append(pm)
                    em.wait("gpsimd", (sV.h, pm))
                    if sgrp == 0:
                        em.wait("gpsimd", gx_loaded[n][1])
                        if n >= 2:
                            em.wait("gpsimd", (sOUT[par].h, 16 * out_cnt[par]))
                    em.add("gpsimd", lambda e, s=s, half=half, par=par:
                           e.tensor_tensor(o_stage[par][:, s, half * 512:(half + 1) * 512], prod[:],
                                           x_win[par][:, s, half * 512:(half + 1) * 512],
                                           ALU.add).then_inc(sP.h, 1))
                    tail_epi[n][1].append(sP.inc())
                    rel_prod[0] = tail_epi[n][1][-1]

                def tail_finish(n):
                    par = n % 2
                    epi_done[n] = (max(tail_epi[n][0]), max(tail_epi[n][1]))
                    em.wait("sync", (sP.h, epi_done[n][1]))
                    em.add("sync", lambda e, n=n, par=par:
                           e.dma_start(out=out[n * 256:(n + 1) * 256, :].rearrange("(a p) e -> p a e", p=128),
                                       in_=o_stage[par][:]).then_inc(sOUT[par].h, 16))
                    out_cnt[par] += 1
                    sOUT[par].inc(16)
                    if n + 2 < NW:
                        prefetch(n + 2)

                def emit_window_tail(n):
                    tail_epi[n] = ([], [])
                    for sgrp in range(4):
                        tail_jobs.append(lambda n=n, sgrp=sgrp: tail_group(n, sgrp))
                    tail_jobs.append(lambda n=n: tail_finish(n))

                for slot in range(NHH + 3):
                    if 2 <= slot <= NHH + 1:
                        emit_V(slot - 2)
                    if slot >= 3:
                        emit_R(slot - 3)
                    if slot < NHH:
                        emit_F(slot)
                    if slot >= 3:
                        nb, hb = divmod(slot - 3, H)
                        if hb == H - 1:
                            emit_window_tail(nb)
                    if tail_jobs:
                        tail_jobs.pop(0)()
                while tail_jobs:
                    tail_jobs.pop(0)()

                em.wait("sync", (sOUT[0].h, sOUT[0].n))
                em.wait("sync", (sOUT[1].h, sOUT[1].n))

                with nc.Block("attn") as blk:
                    em.run(blk)

    return nc


_BUILD_CACHE = {}


def _get_nc():
    if "nc" not in _BUILD_CACHE:
        _BUILD_CACHE["nc"] = build_nc()
    return _BUILD_CACHE["nc"]


def make_in_maps(x, wq1, wq2, wk1, wk2, wv_, wout_, bout_, lvec, wg, wu, wd):
    iden = np.eye(128, dtype=np.float32)
    in_maps = []
    for core in range(NCORE):
        b, qq = core // 4, core % 4
        cos, sin = _core_tables(qq)
        in_maps.append(dict(
            x_attn=np.ascontiguousarray(x[b, 2048 * qq:2048 * qq + AW, :]),
            x_ffn=np.ascontiguousarray(x[b, 4096 * qq:4096 * (qq + 1), :]),
            wq1=wq1, wq2=wq2, wk1=wk1, wk2=wk2, wv=wv_, wout=wout_, bout=bout_,
            lvec=lvec, wg_i=wg, wu_i=wu, wd_i=wd,
            tcos=cos, tsin=sin, paT=_PAT, pbT=_PBT, iden=iden,
        ))
    return in_maps


def kernel(L0_attn_x, L0_attn_Wq1, L0_attn_Wq2, L0_attn_Wk1, L0_attn_Wk2,
           L0_attn_Wv, L0_attn_Wout, L0_attn_bout, L0_attn_lq1, L0_attn_lk1,
           L0_attn_lq2, L0_attn_lk2, L0_ffn_w_gate, L0_ffn_w_up, L0_ffn_w_down):
    x = np.ascontiguousarray(np.asarray(L0_attn_x, dtype=np.float32))
    f32 = lambda a: np.ascontiguousarray(np.asarray(a, dtype=np.float32))
    i32 = lambda a: np.ascontiguousarray(np.asarray(a, dtype=np.int32))
    lvec = np.concatenate([f32(L0_attn_lq1).ravel(), f32(L0_attn_lk1).ravel(),
                           f32(L0_attn_lq2).ravel(), f32(L0_attn_lk2).ravel()]).reshape(1, 256)
    nc = _get_nc()
    in_maps = make_in_maps(x, f32(L0_attn_Wq1), f32(L0_attn_Wq2), f32(L0_attn_Wk1),
                           f32(L0_attn_Wk2), f32(L0_attn_Wv), f32(L0_attn_Wout),
                           f32(L0_attn_bout).reshape(1, E), lvec,
                           i32(L0_ffn_w_gate), i32(L0_ffn_w_up), i32(L0_ffn_w_down))
    res = run_bass_kernel_spmd(nc, in_maps, core_ids=list(range(NCORE)))
    outp = np.empty((B, L, E), dtype=np.float32)
    for core in range(NCORE):
        b, qq = core // 4, core % 4
        outp[b, 4096 * qq:4096 * (qq + 1), :] = res.results[core]["out"]
    return outp


# version 27
# speedup vs baseline: 1.2042x; 1.0357x over previous
import numpy as np
from contextlib import ExitStack

import concourse.bass as bass
import concourse.mybir as mybir
from concourse.bass_utils import run_bass_kernel_spmd

F32 = mybir.dt.float32
BF16 = mybir.dt.bfloat16
I32 = mybir.dt.int32
AF = mybir.ActivationFunctionType
ALU = mybir.AluOpType
AX = mybir.AxisListType

B, L, E, H, G, W, D = 2, 16384, 1024, 16, 2, 256, 64
HID = 3072
NCORE = 8
QPC = 4096
NW = 16
AW = 2176
ACH = AW // 128
PCH = QPC // 128
NCH = PCH + ACH
TGRP = 512
NGRP = QPC // TGRP
CPG = TGRP // 128
U_SHIFT = 12.0
EPS = 1e-5
INV127 = 1.0 / 127.0
INV127SQ = 1.0 / (127.0 * 127.0)
TT_SPLITS = [(0, 512), (512, 512), (1024, 512), (1536, 512), (2048, 128)]


class Sem:
    def __init__(self, h):
        self.h = h
        self.n = 0

    def inc(self, k=1):
        self.n += k
        return self.n


class Em:
    ENGS = ("sync", "gpsimd", "tensor", "vector", "scalar")

    def __init__(self):
        self.q = {e: [] for e in self.ENGS}

    def add(self, eng, fn):
        self.q[eng].append(fn)

    def wait(self, eng, tok):
        if tok is not None:
            self.add(eng, lambda e, s=tok[0], v=tok[1]: e.wait_ge(s, v))

    def run(self, blk):
        for eng in self.ENGS:
            fns = self.q[eng]
            getattr(blk, eng)(lambda e, fns=fns: [f(e) for f in fns])


def _rope_tables():
    dpos = np.arange(0, D, 2, dtype=np.float64)
    inv = np.exp(dpos * (-np.log(10000.0) / D))
    pa = np.zeros((64, 64), np.float32)
    pb = np.zeros((64, 64), np.float32)
    for dp in range(32):
        pa[dp, 2 * dp] = 1.0
        pa[32 + dp, 2 * dp + 1] = 1.0
        pb[dp, 2 * dp + 1] = -1.0
        pb[32 + dp, 2 * dp] = 1.0
    z = np.zeros_like(pa)
    pa128 = np.block([[pa, z], [z, pa]])
    pb128 = np.block([[pb, z], [z, pb]])
    paT = np.ascontiguousarray(pa128.T)
    pbT = np.ascontiguousarray(pb128.T)
    freqs128 = np.concatenate([inv, inv, inv, inv])
    return paT, pbT, freqs128


_PAT, _PBT, _FREQS128 = _rope_tables()


def _core_tables(qq):
    a0 = 2048 * qq
    pos = (a0 + np.arange(AW, dtype=np.float64))[None, :]
    ang = _FREQS128[:, None] * pos
    return np.cos(ang).astype(np.float32), np.sin(ang).astype(np.float32)


def build_nc(stages=3):
    nc = bass.Bass("TRN2", target_bir_lowering=False)
    ap = lambda name, shape, dt, kind: nc.dram_tensor(name, shape, dt, kind=kind).ap()

    x_attn = ap("x_attn", [AW, E], F32, "ExternalInput")
    x_ffn = ap("x_ffn", [QPC, E], F32, "ExternalInput")
    wq1 = ap("wq1", [E, E], F32, "ExternalInput")
    wq2 = ap("wq2", [E, E], F32, "ExternalInput")
    wk1 = ap("wk1", [E, 128], F32, "ExternalInput")
    wk2 = ap("wk2", [E, 128], F32, "ExternalInput")
    wv = ap("wv", [E, 128], F32, "ExternalInput")
    wout = ap("wout", [E, E], F32, "ExternalInput")
    bout = ap("bout", [1, E], F32, "ExternalInput")
    lvec = ap("lvec", [1, 256], F32, "ExternalInput")
    wg_i = ap("wg_i", [HID, E], I32, "ExternalInput")
    wu_i = ap("wu_i", [HID, E], I32, "ExternalInput")
    wd_i = ap("wd_i", [E, HID], I32, "ExternalInput")
    tcos = ap("tcos", [128, AW], F32, "ExternalInput")
    tsin = ap("tsin", [128, AW], F32, "ExternalInput")
    paT = ap("paT", [128, 128], F32, "ExternalInput")
    pbT = ap("pbT", [128, 128], F32, "ExternalInput")
    iden = ap("iden", [128, 128], F32, "ExternalInput")

    out = ap("out", [QPC, E], F32, "ExternalOutput")

    scr_gu = ap("scr_gu", [24 * 128, 2048], BF16, "Internal")
    scr_g = ap("scr_g", [QPC, E], F32, "Internal")

    with ExitStack() as ctx:
        sb = lambda n, s, d: ctx.enter_context(nc.sbuf_tensor(n, s, d))
        mksem = lambda n: Sem(ctx.enter_context(nc.semaphore(n)))

        ab_scale = sb("ab_scale", [128, NCH], F32)
        ab_bias = sb("ab_bias", [128, NCH], F32)
        ident_sb = sb("ident_sb", [128, 128], BF16)
        ubias = sb("ubias", [128, 1], F32)
        zbias = sb("zbias", [128, 1], F32)
        epsb = sb("epsb", [128, 1], F32)
        ones1 = sb("ones1", [1, 128], F32)

        sT = mksem("sT")
        sV = mksem("sV")
        sC = mksem("sC")
        sP = mksem("sP")
        sK0 = mksem("sK0")
        sWPL = [mksem(f"sWPL{i}") for i in range(3)]
        sWPS = [mksem(f"sWPS{i}") for i in range(2)]
        sFX = [mksem(f"sFX{i}") for i in range(2)]
        sWT = [mksem(f"sWT{i}") for i in range(2)]
        sGW = mksem("sGW")
        sK2 = mksem("sK2")
        sK2S = mksem("sK2S")
        sXA = [mksem(f"sXA{i}") for i in range(4)]
        sGR = [mksem(f"sGR{i}") for i in range(2)]
        sXW = [mksem(f"sXW{i}") for i in range(2)]
        sOUT = [mksem(f"sOUT{i}") for i in range(2)]

        c01 = ctx.enter_context(ExitStack())
        wdT_sb = c01.enter_context(nc.sbuf_tensor("wdT_sb", [128, 24 * E], BF16))
        wdT_sb3 = wdT_sb[:].rearrange("p (a b) -> p a b", b=E)
        with ExitStack() as c0:
            sb0 = lambda n, s, d: c0.enter_context(nc.sbuf_tensor(n, s, d))
            ps0 = lambda n, s, d: c0.enter_context(nc.psum_tensor(n, s, d))
            wrow = [sb0(f"wrow{i}", [128, HID], BF16) for i in range(3)]
            stag = [sb0(f"stag{i}", [128, 8, 128], BF16) for i in range(2)]
            pstr = [ps0(f"pstr{i}", [128, 128], BF16) for i in range(6)]

            em = Em()

            em.add("vector", lambda e: e.memset(ubias[:], -U_SHIFT))
            em.add("vector", lambda e: e.memset(zbias[:], 0.0))
            em.add("vector", lambda e: e.memset(epsb[:], EPS))
            em.add("vector", lambda e: e.memset(ones1[:], 1.0).then_inc(sV.h, 1))
            consts_done = sV.inc()
            em.add("gpsimd", lambda e: e.dma_start(out=ident_sb[:], in_=iden).then_inc(sK0.h, 16))
            ident_ld = sK0.inc(16)

            prep_trel = {}
            prep_srel = {}
            trevac = {}
            evn = [0]
            em.wait("scalar", (sV.h, consts_done))

            def prep_matrix(src, n_m, base_pi, store_cb=None, dst_cb=None):
                for m in range(n_m):
                    pi = base_pi + m
                    buf = wrow[pi % 3]
                    stg = stag[pi % 2]
                    wbytes = src.shape[1]
                    if pi >= 3:
                        em.wait("gpsimd", (sT.h, prep_trel[pi - 3]))
                    em.add("gpsimd", lambda e, b=buf, s=src, m=m, w=wbytes, q=pi % 3:
                           e.dma_start(out=b[:, 0:w], in_=s[m * 128:(m + 1) * 128, :])
                           .then_inc(sWPL[q].h, 16))
                    ld = (sWPL[pi % 3].h, sWPL[pi % 3].inc(16))
                    em.wait("tensor", ld)
                    if pi == 0:
                        em.wait("tensor", (sK0.h, ident_ld))
                    nk = src.shape[1] // 128
                    war_stg = prep_srel.get(pi - 2) if store_cb is not None else None
                    for k in range(nk):
                        bank = pstr[evn[0] % 6]
                        if evn[0] >= 6:
                            em.wait("tensor", trevac[evn[0] - 6])
                        em.add("tensor", lambda e, b=bank, w=buf, k=k:
                               e.transpose(b[:], w[:, k * 128:(k + 1) * 128],
                                           ident_sb[:]).then_inc(sT.h, 1))
                        tc = sT.inc()
                        dst = stg[:, k, :] if store_cb is not None else dst_cb(m, k)
                        if evn[0] % 2 == 0:
                            em.wait("vector", (sT.h, tc))
                            if k <= 1 and war_stg is not None:
                                em.wait("vector", war_stg)
                            em.add("vector", lambda e, b=bank, d=dst:
                                   e.tensor_copy(d, b[:]).then_inc(sV.h, 1))
                            trevac[evn[0]] = (sV.h, sV.inc())
                        else:
                            em.wait("scalar", (sT.h, tc))
                            if k <= 1 and war_stg is not None:
                                em.wait("scalar", war_stg)
                            em.add("scalar", lambda e, b=bank, d=dst:
                                   e.activation(d, b[:], AF.Identity, bias=zbias[:],
                                                scale=1.0).then_inc(sC.h, 1))
                            trevac[evn[0]] = (sC.h, sC.inc())
                        evn[0] += 1
                    prep_trel[pi] = sT.n
                    if store_cb is not None:
                        em.wait("sync", trevac[evn[0] - 1])
                        em.wait("sync", trevac[evn[0] - 2])
                        em.add("sync", lambda e, s=stg, m=m, cb=store_cb, q=pi % 2:
                               e.dma_start(out=cb(m), in_=s[:]).then_inc(sWPS[q].h, 16))
                        prep_srel[pi] = (sWPS[pi % 2].h, sWPS[pi % 2].inc(16))

            prep_matrix(wg_i, 24, 0,
                        store_cb=lambda m: scr_gu[m * 128:(m + 1) * 128, 0:1024])
            prep_matrix(wu_i, 24, 24,
                        store_cb=lambda m: scr_gu[m * 128:(m + 1) * 128, 1024:2048])
            prep_matrix(wd_i, 8, 48,
                        dst_cb=lambda m, k: wdT_sb3[:, k, m * 128:(m + 1) * 128])

            b0_wps = (sWPS[0].n, sWPS[1].n)
            with nc.Block("prep") as blk:
                em.run(blk)

        if stages >= 2:
          with ExitStack() as c1:
            sb1 = lambda n, s, d: c1.enter_context(nc.sbuf_tensor(n, s, d))
            ps1 = lambda n, s, d: c1.enter_context(nc.psum_tensor(n, s, d))
            xc1 = [sb1(f"xc1_{i}", [128, E], F32) for i in range(2)]
            bnb = sb1("bnb", [128, 2, 12], F32)
            mvb = sb1("mvb", [128, 2, 2], F32)
            lnc = sb1("lnc", [128, 2], F32)
            tmpc = sb1("tmpc", [128, 2], F32)
            xn_tm = sb1("xn_tm", [128, CPG, E], BF16)
            xn_fm = [sb1(f"xn_fm{i}", [128, 8, TGRP], BF16) for i in range(2)]
            wgut = [sb1(f"wgut{i}", [128, 16, 128], BF16) for i in range(2)]
            silu_g = [sb1(f"silu_g{i}", [128, TGRP], BF16) for i in range(2)]
            h_sb = sb1("h_sb", [128, 24, TGRP], BF16)
            g_stage = [sb1(f"g_stage{i}", [128, E], F32) for i in range(2)]
            ps_xt = [ps1(f"ps_xt{i}", [128, 128], BF16) for i in range(2)]
            ps_gu = [ps1(f"ps_gu{i}", [128, TGRP], F32) for i in range(4)]
            ps_dn = [ps1(f"ps_dn{i}", [128, TGRP], F32) for i in range(2)]

            em = Em()
            em.wait("sync", (sWPS[0].h, b0_wps[0]))
            em.wait("sync", (sWPS[1].h, b0_wps[1]))
            em.wait("sync", (sWPL[0].h, sWPL[0].n))
            em.wait("sync", (sWPL[1].h, sWPL[1].n))
            em.wait("sync", (sWPL[2].h, sWPL[2].n))
            em.wait("sync", (sK0.h, sK0.n))

            xe_done = {}
            ln1_done = {}

            def chunk_stats(ch, par, buf, ld):
                em.wait("vector", ld)
                if ch - 2 in ln1_done:
                    em.wait("vector", (sC.h, ln1_done[ch - 2]))
                em.add("vector", lambda e, b=buf, par=par:
                       e.bn_stats(bnb[:, par, 0:6], b[:, 0:512]))
                em.add("vector", lambda e, b=buf, par=par:
                       e.bn_stats(bnb[:, par, 6:12], b[:, 512:1024]).then_inc(sV.h, 1))
                em.wait("vector", (sV.h, sV.inc()))
                em.add("vector", lambda e, par=par:
                       e.bn_aggr(mvb[:, par, :], bnb[:, par, :]).then_inc(sV.h, 1))
                aggr = sV.inc()
                em.wait("scalar", (sV.h, aggr))
                em.add("scalar", lambda e, par=par:
                       e.activation(lnc[:, par:par + 1], mvb[:, par, 1:2], AF.Ln,
                                    bias=epsb[:], scale=1.0).then_inc(sC.h, 1))
                ln1_done[ch] = sC.inc()
                em.wait("scalar", (sC.h, ln1_done[ch]))
                em.add("scalar", lambda e, ch=ch, par=par:
                       e.activation(ab_scale[:, ch:ch + 1], lnc[:, par:par + 1], AF.Exp,
                                    bias=zbias[:], scale=-0.5).then_inc(sC.h, 1))
                rstd = sC.inc()
                em.wait("vector", (sC.h, rstd))
                em.add("vector", lambda e, ch=ch, par=par:
                       e.tensor_tensor(tmpc[:, par:par + 1], mvb[:, par, 0:1],
                                       ab_scale[:, ch:ch + 1], ALU.mult).then_inc(sV.h, 1))
                em.wait("vector", (sV.h, sV.inc()))
                em.add("vector", lambda e, ch=ch, par=par:
                       e.tensor_scalar(ab_bias[:, ch:ch + 1], tmpc[:, par:par + 1],
                                       -1.0, None, ALU.mult).then_inc(sV.h, 1))
                return sV.inc()
            mm_done = {}
            sg_done = {}
            h_done = {}
            dn_done = {}
            g_done = {}
            wld = {}
            wrel = {}
            xtrel = {}
            xt_last = {}
            gt = [0]

            def emit_xn_chunk(grp, cc):
                ch = grp * CPG + cc
                par = ch % 2
                buf = xc1[par]
                if ch >= 2:
                    em.wait("sync", (sC.h, xe_done[ch - 2]))
                em.add("sync", lambda e, b=buf, ch=ch, par=par:
                       e.dma_start(out=b[:], in_=x_ffn[ch * 128:(ch + 1) * 128, :])
                       .then_inc(sFX[par].h, 16))
                ld = (sFX[par].h, sFX[par].inc(16))
                bias_ok = chunk_stats(ch, par, buf, ld)
                em.wait("scalar", ld)
                em.wait("scalar", (sV.h, bias_ok))
                if ch >= 4:
                    em.wait("scalar", (sV.h, xtrel[(ch - 4) * 8 + 7]))
                em.add("scalar", lambda e, b=buf, cc=cc, ch=ch:
                       e.activation(xn_tm[:, cc, :], b[:], AF.Identity,
                                    bias=ab_bias[:, ch:ch + 1],
                                    scale=ab_scale[:, ch:ch + 1]).then_inc(sC.h, 1))
                xe_done[ch] = sC.inc()

            def emit_transpose_pair(grp, t2):
                fm = xn_fm[grp % 2]
                for t in (t2, t2 + 1):
                    cc, k = divmod(t, 8)
                    g = gt[0]
                    j = g % 2
                    if t % 8 == 0:
                        em.wait("tensor", (sC.h, xe_done[grp * CPG + cc]))
                    if g >= 2:
                        em.wait("tensor", (sV.h, xtrel[g - 2]))
                    em.add("tensor", lambda e, cc=cc, k=k, j=j:
                           e.transpose(ps_xt[j][:], xn_tm[:, cc, k * 128:(k + 1) * 128],
                                       ident_sb[:]).then_inc(sT.h, 1))
                    tc = sT.inc()
                    em.wait("vector", (sT.h, tc))
                    em.add("vector", lambda e, fm=fm, cc=cc, k=k, j=j:
                           e.tensor_copy(fm[:, k, cc * 128:(cc + 1) * 128],
                                         ps_xt[j][:]).then_inc(sV.h, 1))
                    xtrel[g] = sV.inc()
                    gt[0] += 1
                if t2 + 2 == 32:
                    xt_last[grp] = sV.n

            for cc in range(CPG):
                emit_xn_chunk(0, cc)
            for t2 in range(0, 32, 2):
                emit_transpose_pair(0, t2)

            for grp in range(NGRP):
                fm = xn_fm[grp % 2]
                for m in range(24):
                    if grp + 1 < NGRP:
                        if m % 2 == 0 and m < 8:
                            emit_xn_chunk(grp + 1, m // 2)
                        if m >= 8:
                            emit_transpose_pair(grp + 1, (m - 8) * 2)
                    par = m % 2
                    pkey = (grp, m - 2) if m >= 2 else ((grp - 1, 22 + m) if grp > 0 else None)
                    if pkey is not None:
                        em.wait("sync", (sT.h, wrel[pkey]))
                    em.add("sync", lambda e, m=m, par=par:
                           e.dma_start(out=wgut[par][:],
                                       in_=scr_gu[m * 128:(m + 1) * 128, :])
                           .then_inc(sWT[par].h, 16))
                    wld[(grp, m)] = (sWT[par].h, sWT[par].inc(16))
                    em.wait("tensor", wld[(grp, m)])
                    if m == 0:
                        em.wait("tensor", (sV.h, xt_last[grp]))
                    if pkey is not None:
                        em.wait("tensor", (sC.h, sg_done[pkey]))
                        em.wait("tensor", (sV.h, h_done[pkey]))
                    if m <= 1 and grp > 0:
                        em.wait("tensor", (sC.h, g_done[(grp - 1, 3)]))
                    wt = wgut[par]
                    for k in range(8):
                        em.add("tensor", lambda e, wt=wt, k=k, par=par, fm=fm:
                               e.matmul(ps_gu[par][:], wt[:, k, :], fm[:, k, :],
                                        start=(k == 0), stop=(k == 7)))
                    for k in range(8):
                        last = k == 7
                        em.add("tensor", lambda e, wt=wt, k=k, par=par, fm=fm, last=last:
                               (e.matmul(ps_gu[2 + par][:], wt[:, 8 + k, :], fm[:, k, :],
                                         start=(k == 0), stop=last).then_inc(sT.h, 1)
                                if last else
                                e.matmul(ps_gu[2 + par][:], wt[:, 8 + k, :], fm[:, k, :],
                                         start=(k == 0), stop=last)))
                    mm_done[(grp, m)] = sT.inc()
                    wrel[(grp, m)] = mm_done[(grp, m)]
                    em.wait("scalar", (sT.h, mm_done[(grp, m)]))
                    if pkey is not None:
                        em.wait("scalar", (sV.h, h_done[pkey]))
                    em.add("scalar", lambda e, par=par:
                           e.activation(silu_g[par][:], ps_gu[par][:], AF.Silu,
                                        bias=zbias[:], scale=INV127).then_inc(sC.h, 1))
                    sg_done[(grp, m)] = sC.inc()
                    em.wait("vector", (sC.h, sg_done[(grp, m)]))
                    em.add("vector", lambda e, m=m, par=par:
                           e.tensor_tensor(h_sb[:, m, :], ps_gu[2 + par][:], silu_g[par][:],
                                           ALU.mult).then_inc(sV.h, 1))
                    h_done[(grp, m)] = sV.inc()

                for tt in range(CPG):
                    par = tt % 2
                    pair = (ps_dn[0], ps_dn[1]) if tt % 2 == 0 else (ps_gu[0], ps_gu[1])
                    if tt >= 2:
                        em.wait("tensor", (sC.h, g_done[(grp, tt - 2)]))
                    elif tt == 0:
                        if grp > 0:
                            em.wait("tensor", (sC.h, g_done[(grp - 1, 2)]))
                        em.wait("tensor", (sV.h, h_done[(grp, 23)]))
                    else:
                        em.wait("tensor", (sC.h, sg_done[(grp, 23)]))
                    for half in range(2):
                        for k in range(24):
                            last = (k == 23 and half == 1)
                            em.add("tensor", lambda e, tt=tt, k=k, half=half,
                                   pair=pair, last=last:
                                   (e.matmul(pair[half][:], h_sb[:, k, tt * 128:(tt + 1) * 128],
                                             wdT_sb3[:, k, half * 512:(half + 1) * 512],
                                             start=(k == 0), stop=(k == 23)).then_inc(sT.h, 1)
                                    if last else
                                    e.matmul(pair[half][:], h_sb[:, k, tt * 128:(tt + 1) * 128],
                                             wdT_sb3[:, k, half * 512:(half + 1) * 512],
                                             start=(k == 0), stop=(k == 23))))
                    dn_done[(grp, tt)] = sT.inc()
                    em.wait("scalar", (sT.h, dn_done[(grp, tt)]))
                    if sGW.n >= 32:
                        em.wait("scalar", (sGW.h, sGW.n))
                    for half in range(2):
                        last = half == 1
                        em.add("scalar", lambda e, half=half, par=par, pair=pair, last=last:
                               e.activation(g_stage[par][:, half * 512:(half + 1) * 512],
                                            pair[half][:], AF.Silu,
                                            bias=zbias[:], scale=INV127SQ).then_inc(sC.h, 1))
                        sC.inc()
                    g_done[(grp, tt)] = sC.n
                    em.wait("scalar", (sC.h, g_done[(grp, tt)]))
                    r0 = (grp * CPG + tt) * 128
                    em.add("scalar", lambda e, r0=r0, par=par:
                           e.dma_start(out=scr_g[r0:r0 + 128, :],
                                       in_=g_stage[par][:]).then_inc(sGW.h, 16))
                    sGW.inc(16)

            b1_gw = sGW.n
            with nc.Block("ffn") as blk:
                em.run(blk)
        c01.close()

        if stages >= 3:
          with ExitStack() as c2:
            sb2 = lambda n, s, d: c2.enter_context(nc.sbuf_tensor(n, s, d))
            q1r = sb2("q1r", [128, 8, AW], BF16)
            q2r = sb2("q2r", [128, 8, AW], BF16)
            k1r = sb2("k1r", [128, AW], BF16)
            k2r = sb2("k2r", [128, AW], BF16)
            v0_tm = sb2("v0_tm", [128, ACH, 128], BF16)
            v1_tm = sb2("v1_tm", [128, ACH, 128], BF16)
            lamt = sb2("lamt", [128, 8], F32)

            em = Em()
            for eng in ("sync", "gpsimd"):
                em.wait(eng, (sGW.h, b1_gw))
                em.wait(eng, (sWT[0].h, sWT[0].n))
                em.wait(eng, (sWT[1].h, sWT[1].n))
                em.wait(eng, (sFX[0].h, sFX[0].n))
                em.wait(eng, (sFX[1].h, sFX[1].n))

            em.add("vector", lambda e: e.memset(v0_tm[:], 0.0))
            em.add("vector", lambda e: e.memset(v1_tm[:], 0.0).then_inc(sV.h, 1))
            em.wait("vector", (sV.h, sV.inc()))
            em.add("vector", lambda e: e.memset(v0_tm[:, :, 64:65], 1.0))
            em.add("vector", lambda e: e.memset(v1_tm[:, :, 0:1], 1.0).then_inc(sV.h, 1))
            vinit_done = sV.inc()

            with ExitStack() as cp:
                sbp = lambda n, s, d: cp.enter_context(nc.sbuf_tensor(n, s, d))
                psp = lambda n, s, d: cp.enter_context(nc.psum_tensor(n, s, d))
                wq1_sb = sbp("wq1_sb", [128, 8, E], BF16)
                wq2_sb = sbp("wq2_sb", [128, 8, E], BF16)
                wk1_sb = sbp("wk1_sb", [128, 8, 128], BF16)
                wk2_sb = sbp("wk2_sb", [128, 8, 128], BF16)
                wv_sb = sbp("wv_sb", [128, 8, 128], BF16)
                paT_sb = sbp("paT_sb", [128, 128], BF16)
                pbT_sb = sbp("pbT_sb", [128, 128], BF16)
                cos_sb = sbp("cos_sb", [128, AW], F32)
                sin_sb = sbp("sin_sb", [128, AW], F32)
                xna_fm = sbp("xna_fm", [128, 8, AW], BF16)
                qf = sbp("qf", [128, 2, 512], BF16)
                rt1 = sbp("rt1", [128, 2, 512], BF16)
                rt2 = sbp("rt2", [128, 2, 512], BF16)
                lv_sb = sbp("lv_sb", [1, 256], F32)
                dots = sbp("dots", [1, 130], F32)
                ps_p = [psp(f"ps_p{i}", [128, 512], F32) for i in range(2)]
                ps_ra = psp("ps_ra", [128, 2, 512], F32)
                ps_rb = psp("ps_rb", [128, 2, 512], F32)
                ps_vt = [psp(f"ps_vt{i}", [128, 128], BF16) for i in range(2)]
                ps_lam = ps_ra[:, 0, 0:2]

                for t_ap, t_sb in [(wq1, wq1_sb), (wq2, wq2_sb), (wk1, wk1_sb),
                                   (wk2, wk2_sb), (wv, wv_sb)]:
                    em.add("gpsimd", lambda e, a=t_ap, s=t_sb:
                           e.dma_start(out=s[:], in_=a.rearrange("(k p) n -> p k n", p=128))
                           .then_inc(sK2.h, 16))
                    sK2.inc(16)
                em.add("gpsimd", lambda e: e.dma_start(out=paT_sb[:], in_=paT).then_inc(sK2.h, 16))
                em.add("gpsimd", lambda e: e.dma_start(out=pbT_sb[:], in_=pbT).then_inc(sK2.h, 16))
                em.add("gpsimd", lambda e: e.dma_start(out=cos_sb[:], in_=tcos).then_inc(sK2.h, 16))
                em.add("gpsimd", lambda e: e.dma_start(out=sin_sb[:], in_=tsin).then_inc(sK2.h, 16))
                sK2.inc(16 * 4)
                em.add("sync", lambda e: e.dma_start(out=lv_sb[:], in_=lvec).then_inc(sK2S.h, 16))
                lv_ld = sK2S.inc(16)
                wload = sK2.n

                em.wait("vector", (sK2S.h, lv_ld))
                em.add("vector", lambda e: e.tensor_tensor(dots[:, 0:64], lv_sb[:, 0:64],
                                                           lv_sb[:, 64:128], ALU.mult))
                em.add("vector", lambda e: e.tensor_tensor(dots[:, 64:128], lv_sb[:, 128:192],
                                                           lv_sb[:, 192:256], ALU.mult).then_inc(sV.h, 1))
                dmul = sV.inc()
                em.wait("vector", (sV.h, dmul))
                em.add("vector", lambda e: e.tensor_reduce(dots[:, 128:129], dots[:, 0:64], AX.X, ALU.add))
                em.add("vector", lambda e: e.tensor_reduce(dots[:, 129:130], dots[:, 64:128],
                                                           AX.X, ALU.add).then_inc(sV.h, 1))
                dred = sV.inc()
                em.wait("tensor", (sV.h, dred))
                em.add("tensor", lambda e: e.matmul(ps_lam, ones1[:], dots[:, 128:130],
                                                    start=True, stop=True).then_inc(sT.h, 1))
                dbc = sT.inc()
                em.wait("scalar", (sT.h, dbc))
                em.add("scalar", lambda e: e.activation(lamt[:, 2:4], ps_lam, AF.Exp,
                                                        bias=zbias[:], scale=1.0).then_inc(sC.h, 1))
                lexp = sC.inc()
                em.wait("vector", (sC.h, lexp))
                em.add("vector", lambda e: e.tensor_tensor(lamt[:, 4:5], lamt[:, 2:3],
                                                           lamt[:, 3:4], ALU.subtract).then_inc(sV.h, 1))
                em.wait("vector", (sV.h, sV.inc()))
                em.add("vector", lambda e: e.tensor_scalar(lamt[:, 5:6], lamt[:, 4:5], 0.8, None,
                                                           ALU.add).then_inc(sV.h, 1))
                em.wait("vector", (sV.h, sV.inc()))
                em.add("vector", lambda e: e.tensor_scalar(lamt[:, 6:7], lamt[:, 5:6], 0.1, 0.9,
                                                           ALU.max, ALU.min).then_inc(sV.h, 1))
                em.wait("vector", (sV.h, sV.inc()))
                em.add("vector", lambda e: e.tensor_scalar(lamt[:, 1:2], lamt[:, 6:7], -1.0, None,
                                                           ALU.mult).then_inc(sV.h, 1))
                lam_done = sV.inc()

                with ExitStack() as cx:
                    xa_tm = cx.enter_context(nc.sbuf_tensor("xa_tm", [128, 4, E], BF16))
                    xa_f32 = [cx.enter_context(nc.sbuf_tensor(f"xa_f32_{i}", [128, E], F32))
                              for i in range(4)]
                    bnb2 = cx.enter_context(nc.sbuf_tensor("bnb2", [128, 2, 12], F32))
                    mvb2 = cx.enter_context(nc.sbuf_tensor("mvb2", [128, 2, 2], F32))
                    lnc2 = cx.enter_context(nc.sbuf_tensor("lnc2", [128, 2], F32))
                    tmpc2 = cx.enter_context(nc.sbuf_tensor("tmpc2", [128, 2], F32))
                    ln2_done = {}

                    def xa_stats(ch, par, buf, ld):
                        em.wait("vector", ld)
                        if ch - 2 in ln2_done:
                            em.wait("vector", (sC.h, ln2_done[ch - 2]))
                        em.add("vector", lambda e, b=buf, par=par:
                               e.bn_stats(bnb2[:, par, 0:6], b[:, 0:512]))
                        em.add("vector", lambda e, b=buf, par=par:
                               e.bn_stats(bnb2[:, par, 6:12], b[:, 512:1024]).then_inc(sV.h, 1))
                        em.wait("vector", (sV.h, sV.inc()))
                        em.add("vector", lambda e, par=par:
                               e.bn_aggr(mvb2[:, par, :], bnb2[:, par, :]).then_inc(sV.h, 1))
                        aggr = sV.inc()
                        em.wait("scalar", (sV.h, aggr))
                        em.add("scalar", lambda e, par=par:
                               e.activation(lnc2[:, par:par + 1], mvb2[:, par, 1:2], AF.Ln,
                                            bias=epsb[:], scale=1.0).then_inc(sC.h, 1))
                        ln2_done[ch] = sC.inc()
                        em.wait("scalar", (sC.h, ln2_done[ch]))
                        em.add("scalar", lambda e, ch=ch, par=par:
                               e.activation(ab_scale[:, ch:ch + 1], lnc2[:, par:par + 1],
                                            AF.Exp, bias=zbias[:], scale=-0.5).then_inc(sC.h, 1))
                        rstd = sC.inc()
                        em.wait("vector", (sC.h, rstd))
                        em.add("vector", lambda e, ch=ch, par=par:
                               e.tensor_tensor(tmpc2[:, par:par + 1], mvb2[:, par, 0:1],
                                               ab_scale[:, ch:ch + 1], ALU.mult).then_inc(sV.h, 1))
                        em.wait("vector", (sV.h, sV.inc()))
                        em.add("vector", lambda e, ch=ch, par=par:
                               e.tensor_scalar(ab_bias[:, ch:ch + 1], tmpc2[:, par:par + 1],
                                               -1.0, None, ALU.mult).then_inc(sV.h, 1))
                        return sV.inc()

                    xa_done = {}
                    xat_done = {}
                    xarel = {}
                    for i in range(ACH):
                        par = i % 4
                        buf = xa_f32[par]
                        if i >= 4:
                            em.wait("sync", (sC.h, xa_done[i - 4]))
                        em.add("sync", lambda e, b=buf, i=i, par=par:
                               e.dma_start(out=b[:], in_=x_attn[i * 128:(i + 1) * 128, :])
                               .then_inc(sXA[par].h, 16))
                        ld = (sXA[par].h, sXA[par].inc(16))
                        ch = PCH + i
                        bias_ok = xa_stats(ch, i % 2, buf, ld)
                        em.wait("scalar", ld)
                        em.wait("scalar", (sV.h, bias_ok))
                        if i >= 4:
                            em.wait("scalar", (sT.h, xat_done[i - 4]))
                        em.add("scalar", lambda e, b=buf, i=i, ch=ch:
                               e.activation(xa_tm[:, i % 4, :], b[:], AF.Identity,
                                            bias=ab_bias[:, ch:ch + 1],
                                            scale=ab_scale[:, ch:ch + 1]).then_inc(sC.h, 1))
                        xa_done[i] = sC.inc()
                        em.wait("tensor", (sC.h, xa_done[i]))
                        for k in range(8):
                            t = i * 8 + k
                            j = t % 2
                            if t >= 2:
                                em.wait("tensor", xarel[t - 2])
                            em.add("tensor", lambda e, i=i, k=k, j=j:
                                   e.transpose(ps_vt[j][:], xa_tm[:, i % 4, k * 128:(k + 1) * 128],
                                               ident_sb[:]).then_inc(sT.h, 1))
                            tc = sT.inc()
                            if t % 2 == 0:
                                em.wait("vector", (sT.h, tc))
                                em.add("vector", lambda e, i=i, k=k, j=j:
                                       e.tensor_copy(xna_fm[:, k, i * 128:(i + 1) * 128],
                                                     ps_vt[j][:]).then_inc(sV.h, 1))
                                xarel[t] = (sV.h, sV.inc())
                            else:
                                em.wait("scalar", (sT.h, tc))
                                em.add("scalar", lambda e, i=i, k=k, j=j:
                                       e.activation(xna_fm[:, k, i * 128:(i + 1) * 128],
                                                    ps_vt[j][:], AF.Identity, bias=zbias[:],
                                                    scale=1.0).then_inc(sC.h, 1))
                                xarel[t] = (sC.h, sC.inc())
                        xat_done[i] = sT.n
                    xna_ready = (sV.n, sC.n)

                    proj_i = [0]
                    psem = {}
                    qf_rel = [None]
                    rope_rel = [None]
                    rope_pool = [None]
                    rope_hist = []
                    rope_mul_hist = []
                    vt_rel = {0: None, 1: None}

                    def project(w_sb, nm, dest, rope):
                        for m in range(nm):
                            for (tb, tn) in TT_SPLITS:
                                it = proj_i[0]
                                sl = it % 2
                                bank = ps_p[it % 2]
                                if psem.get(it % 2) is not None:
                                    em.wait("tensor", (sC.h, psem[it % 2]))
                                if it == 0:
                                    em.wait("tensor", (sK2.h, wload))
                                    em.wait("tensor", (sV.h, xna_ready[0]))
                                    em.wait("tensor", (sC.h, xna_ready[1]))
                                for k in range(8):
                                    last = k == 7
                                    em.add("tensor", lambda e, b=bank, w=w_sb, m=m, k=k, tb=tb,
                                           tn=tn, last=last:
                                           (e.matmul(b[:, 0:tn], w[:, k, m * 128:(m + 1) * 128],
                                                     xna_fm[:, k, tb:tb + tn],
                                                     start=(k == 0), stop=last).then_inc(sT.h, 1)
                                            if last else
                                            e.matmul(b[:, 0:tn], w[:, k, m * 128:(m + 1) * 128],
                                                     xna_fm[:, k, tb:tb + tn],
                                                     start=(k == 0), stop=last)))
                                mmv = sT.inc()
                                em.wait("scalar", (sT.h, mmv))
                                if qf_rel[0] is not None:
                                    em.wait("scalar", (sT.h, qf_rel[0]))
                                em.add("scalar", lambda e, b=bank, tn=tn, sl=sl:
                                       e.activation(qf[:, sl, 0:tn], b[:, 0:tn], AF.Identity,
                                                    bias=zbias[:], scale=1.0).then_inc(sC.h, 1))
                                psem[it % 2] = sC.inc()
                                if not rope:
                                    em.wait("tensor", (sC.h, psem[it % 2]))
                                    for jj in range(tn // 128):
                                        tch = tb // 128 + jj
                                        vb = tch % 2
                                        if vt_rel[vb] is not None:
                                            em.wait("tensor", (sV.h, vt_rel[vb]))
                                        em.add("tensor", lambda e, jj=jj, sl=sl, vb=vb:
                                               e.transpose(ps_vt[vb][:], qf[:, sl, jj * 128:(jj + 1) * 128],
                                                           ident_sb[:]).then_inc(sT.h, 1))
                                        tc2 = sT.inc()
                                        em.wait("vector", (sT.h, tc2))
                                        if tch == 0:
                                            em.wait("vector", (sV.h, vinit_done))
                                        em.add("vector", lambda e, tch=tch, vb=vb:
                                               e.tensor_copy(v0_tm[:, tch, 0:64], ps_vt[vb][:, 0:64]))
                                        em.add("vector", lambda e, tch=tch, vb=vb:
                                               e.tensor_copy(v1_tm[:, tch, 64:128],
                                                             ps_vt[vb][:, 64:128]).then_inc(sV.h, 1))
                                        vt_rel[vb] = sV.inc()
                                    qf_rel[0] = sT.n
                                else:
                                    em.wait("tensor", (sC.h, psem[it % 2]))
                                    rp2 = it % 2
                                    if it == 0:
                                        em.wait("tensor", (sC.h, lexp))
                                    if len(rope_mul_hist) >= 2:
                                        em.wait("tensor", (sV.h, rope_mul_hist[-2]))
                                    em.add("tensor", lambda e, tn=tn, sl=sl, rp2=rp2:
                                           e.matmul(ps_ra[:, rp2, 0:tn], paT_sb[:], qf[:, sl, 0:tn],
                                                    start=True, stop=True))
                                    em.add("tensor", lambda e, tn=tn, sl=sl, rp2=rp2:
                                           e.matmul(ps_rb[:, rp2, 0:tn], pbT_sb[:], qf[:, sl, 0:tn],
                                                    start=True, stop=True).then_inc(sT.h, 1))
                                    qf_rel[0] = sT.inc()
                                    em.wait("vector", (sT.h, qf_rel[0]))
                                    if rope_pool[0] is not None and len(rope_hist) >= 2:
                                        em.wait("vector", (sP.h, rope_hist[-2]))
                                    em.add("vector", lambda e, tb=tb, tn=tn, rp2=rp2:
                                           e.tensor_tensor(rt1[:, rp2, 0:tn], ps_ra[:, rp2, 0:tn],
                                                           cos_sb[:, tb:tb + tn], ALU.mult))
                                    em.add("vector", lambda e, tb=tb, tn=tn, rp2=rp2:
                                           e.tensor_tensor(rt2[:, rp2, 0:tn], ps_rb[:, rp2, 0:tn],
                                                           sin_sb[:, tb:tb + tn], ALU.mult).then_inc(sV.h, 1))
                                    rope_rel[0] = sV.inc()
                                    rope_mul_hist.append(rope_rel[0])
                                    em.wait("gpsimd", (sV.h, rope_rel[0]))
                                    dst = dest(m, tb, tn)
                                    em.add("gpsimd", lambda e, d=dst, tn=tn, rp2=rp2:
                                           e.tensor_tensor(d, rt1[:, rp2, 0:tn], rt2[:, rp2, 0:tn],
                                                           ALU.add).then_inc(sP.h, 1))
                                    rope_pool[0] = sP.inc()
                                    rope_hist.append(rope_pool[0])
                                proj_i[0] += 1

                    project(wq1_sb, 8, lambda m, tb, tn: q1r[:, m, tb:tb + tn], True)
                    project(wq2_sb, 8, lambda m, tb, tn: q2r[:, m, tb:tb + tn], True)
                    project(wk1_sb, 1, lambda m, tb, tn: k1r[:, tb:tb + tn], True)
                    project(wk2_sb, 1, lambda m, tb, tn: k2r[:, tb:tb + tn], True)
                    project(wv_sb, 1, None, False)
                    em.wait("tensor", (sP.h, sP.n))
                    em.wait("tensor", (sV.h, sV.n))
            proj_end = (sT.n, sV.n, sC.n, sP.n)

            with ExitStack() as cw:
                sbw = lambda n, s, d: cw.enter_context(nc.sbuf_tensor(n, s, d))
                psw = lambda n, s, d: cw.enter_context(nc.psum_tensor(n, s, d))
                wout_sb = sbw("wout_sb", [128, 8, E], BF16)
                bout_sb = sbw("bout_sb", [1, E], F32)
                pe_sb = [sbw(f"pe_sb{p}", [128, 2, 512], BF16) for p in range(2)]
                rz32_sb = sbw("rz32_sb", [128, 2, 512], F32)
                rz_sb = sbw("rz_sb", [128, 2, 512], BF16)
                asb_e = [sbw(f"asb_e{i}", [128, 512], BF16) for i in range(2)]
                tcmb = [sbw(f"tcmb{i}", [128, 512], BF16) for i in range(2)]
                onesq = sbw("onesq", [128, 128], BF16)
                lamt_bf = sbw("lamt_bf", [128, 1], BF16)
                a_sb = [sbw(f"a_sb{i}", [128, 8, 256], BF16) for i in range(2)]
                g_tm = [sbw(f"g_tm{i}", [128, 2, E], F32) for i in range(2)]
                x_win = [sbw(f"x_win{i}", [128, 2, E], F32) for i in range(2)]
                o_stage = [sbw(f"o_stage{i}", [128, 2, E], F32) for i in range(2)]
                prod = sbw("prod", [128, 512], F32)
                ps_s = [[psw(f"ps_s{hp}{p}", [128, 512], F32) for p in range(2)]
                        for hp in range(2)]
                ps_a = psw("ps_a", [128, 512], F32)
                ps_r = psw("ps_r", [128, 2, 512], F32)
                ps_w = psw("ps_w", [128, 512], F32)

                for eng in ("gpsimd", "sync", "scalar"):
                    em.wait(eng, (sT.h, proj_end[0]))
                    em.wait(eng, (sV.h, proj_end[1]))
                    em.wait(eng, (sC.h, proj_end[2]))
                    em.wait(eng, (sP.h, proj_end[3]))
                em.wait("gpsimd", (sK2.h, wload))
                for q in range(4):
                    em.wait("sync", (sXA[q].h, sXA[q].n))
                em.wait("sync", (sK2S.h, lv_ld))

                em.add("gpsimd", lambda e:
                       e.dma_start(out=wout_sb[:], in_=wout.rearrange("(k p) n -> p k n", p=128))
                       .then_inc(sK2.h, 16))
                sK2.inc(16)
                em.add("sync", lambda e: e.dma_start(out=bout_sb[:], in_=bout).then_inc(sK2S.h, 16))
                bout_ld = sK2S.inc(16)
                wb_ld = sK2.n

                em.add("vector", lambda e: e.memset(onesq[:], 1.0).then_inc(sV.h, 1))
                ones_done = sV.inc()
                em.wait("vector", (sV.h, lam_done))
                em.add("vector", lambda e: e.tensor_copy(lamt_bf[:], lamt[:, 1:2]).then_inc(sV.h, 1))
                lambf_done = sV.inc()

                rel_w = [None]
                rel_prod = [None]
                gx_loaded = {}
                epi_done = {}
                out_cnt = [0, 0]

                def prefetch(n):
                    par = n % 2
                    if n == 0:
                        em.wait("scalar", (sGW.h, b1_gw))
                    if n >= 2:
                        em.wait("scalar", (sV.h, epi_done[n - 2][0]))
                        em.wait("sync", (sP.h, epi_done[n - 2][1]))
                    em.add("scalar", lambda e, n=n, par=par:
                           e.dma_start(out=g_tm[par][:],
                                       in_=scr_g[n * 256:(n + 1) * 256, :]
                                       .rearrange("(a p) e -> p a e", p=128))
                           .then_inc(sGR[par].h, 16))
                    sGR[par].inc(16)
                    em.add("sync", lambda e, n=n, par=par:
                           e.dma_start(out=x_win[par][:],
                                       in_=x_ffn[n * 256:(n + 1) * 256, :]
                                       .rearrange("(a p) e -> p a e", p=128)).then_inc(sXW[par].h, 16))
                    sXW[par].inc(16)
                    gx_loaded[n] = ((sGR[par].h, sGR[par].n), (sXW[par].h, sXW[par].n))

                prefetch(0)
                prefetch(1)

                NHH = NW * H
                s1_done = {}
                s2p = {}
                s6_done = {}
                recip_done = {}
                evac_done = {}
                bcast_done = {}
                comb_done = {}

                def emit_F(hh):
                    n, h = divmod(hh, H)
                    hp = hh % 2
                    c, po = h // 2, (h % 2) * 64
                    for p in range(2):
                        qbuf = q1r if p == 0 else q2r
                        kbuf = k1r if p == 0 else k2r
                        if hh >= 2:
                            em.wait("tensor", (sC.h, s2p[(hh - 2, p)]))
                        for kc in range(2):
                            last = kc == 1
                            em.add("tensor", lambda e, q=qbuf, k=kbuf, n=n, kc=kc, po=po,
                                   c=c, p=p, hp=hp, last=last:
                                   (e.matmul(ps_s[hp][p][:, kc * 256:(kc + 1) * 256],
                                             k[po:po + 64, 128 * (n + kc):128 * (n + kc) + 128],
                                             q[po:po + 64, c, 128 * n:128 * n + 256],
                                             start=(kc == 0), stop=last).then_inc(sT.h, 1)
                                    if last else
                                    e.matmul(ps_s[hp][p][:, kc * 256:(kc + 1) * 256],
                                             k[po:po + 64, 128 * (n + kc):128 * (n + kc) + 128],
                                             q[po:po + 64, c, 128 * n:128 * n + 256],
                                             start=(kc == 0), stop=last)))
                        s1_done[(hh, p)] = sT.inc()
                    for p in range(2):
                        em.wait("scalar", (sT.h, s1_done[(hh, p)]))
                        if hh >= 2 and p == 0:
                            em.wait("scalar", (sT.h, s6_done[hh - 2]))
                        em.add("scalar", lambda e, p=p, hp=hp:
                               e.activation(pe_sb[p][:, hp, :], ps_s[hp][p][:], AF.Exp,
                                            bias=ubias[:], scale=0.125).then_inc(sC.h, 1))
                        s2p[(hh, p)] = sC.inc()

                def emit_V(hh):
                    n, h = divmod(hh, H)
                    hp = hh % 2
                    g, po = h % 2, (h % 2) * 64
                    zp = 64 if g == 0 else 0
                    vb = v0_tm if g == 0 else v1_tm
                    mhi = 65 if g == 0 else 128
                    em.wait("tensor", (sC.h, s2p[(hh, 1)]))
                    if hh >= 1:
                        em.wait("tensor", evac_done[hh - 1])
                        em.wait("tensor", recip_done[hh - 1])
                    for p in range(2):
                        for kc in range(2):
                            last = p == 1 and kc == 1
                            em.add("tensor", lambda e, n=n, kc=kc, p=p, hp=hp, vb=vb,
                                   mhi=mhi, last=last:
                                   (e.matmul(ps_a[0:mhi, p * 256:(p + 1) * 256],
                                             vb[:, n + kc, 0:mhi],
                                             pe_sb[p][:, hp, kc * 256:(kc + 1) * 256],
                                             start=(kc == 0), stop=(kc == 1)).then_inc(sT.h, 1)
                                    if last else
                                    e.matmul(ps_a[0:mhi, p * 256:(p + 1) * 256],
                                             vb[:, n + kc, 0:mhi],
                                             pe_sb[p][:, hp, kc * 256:(kc + 1) * 256],
                                             start=(kc == 0), stop=(kc == 1))))
                    s6_done[hh] = sT.inc()
                    rp = (hh // 2) % 2
                    em.wait("scalar", (sT.h, s6_done[hh]))
                    if hh >= 4:
                        em.wait("scalar", (sT.h, bcast_done[hh - 4]))
                    em.add("scalar", lambda e, rp=rp, zp=zp:
                           e.activation(rz32_sb[zp:zp + 1, rp, :], ps_a[zp:zp + 1, :],
                                        AF.Ln, bias=zbias[zp:zp + 1, :],
                                        scale=1.0).then_inc(sC.h, 1))
                    em.wait("scalar", (sC.h, sC.inc()))
                    em.add("scalar", lambda e, rp=rp, zp=zp:
                           e.activation(rz_sb[zp:zp + 1, rp, :], rz32_sb[zp:zp + 1, rp, :],
                                        AF.Exp, bias=zbias[zp:zp + 1, :],
                                        scale=-1.0).then_inc(sC.h, 1))
                    recip_done[hh] = (sC.h, sC.inc())
                    em.wait("vector", (sT.h, s6_done[hh]))
                    if hh >= 2:
                        em.wait("vector", (sV.h, comb_done[hh - 2]))
                    em.add("vector", lambda e, hp=hp, po=po:
                           e.tensor_copy(asb_e[hp][po:po + 64, :],
                                         ps_a[po:po + 64, :]).then_inc(sV.h, 1))
                    evac_done[hh] = (sV.h, sV.inc())

                def emit_R(hh):
                    n, h = divmod(hh, H)
                    hp = hh % 2
                    rp = (hh // 2) % 2
                    c, po = h // 2, (h % 2) * 64
                    zp = 64 if h % 2 == 0 else 0
                    em.wait("tensor", recip_done[hh])
                    if hh == 0:
                        em.wait("tensor", (sV.h, ones_done))
                    if hh >= 2:
                        em.wait("tensor", (sV.h, comb_done[hh - 2]))
                    em.add("tensor", lambda e, hp=hp, rp=rp, zp=zp:
                           e.matmul(ps_r[:, hp, :], onesq[zp:zp + 1, 0:128],
                                    rz_sb[zp:zp + 1, rp, :],
                                    start=True, stop=True).then_inc(sT.h, 1))
                    bcast_done[hh] = sT.inc()
                    em.wait("vector", (sT.h, bcast_done[hh]))
                    em.wait("vector", evac_done[hh])
                    if hh <= 1:
                        em.wait("vector", (sV.h, lambf_done))
                    em.add("vector", lambda e, hp=hp, po=po:
                           e.tensor_tensor(tcmb[hp][po:po + 64, 0:256],
                                           asb_e[hp][po:po + 64, 0:256],
                                           ps_r[po:po + 64, hp, 0:256], ALU.mult))
                    em.add("vector", lambda e, hp=hp, po=po:
                           e.tensor_tensor(tcmb[hp][po:po + 64, 256:512],
                                           asb_e[hp][po:po + 64, 256:512],
                                           ps_r[po:po + 64, hp, 256:512],
                                           ALU.mult).then_inc(sV.h, 1))
                    em.wait("vector", (sV.h, sV.inc()))
                    em.add("vector", lambda e, hp=hp, po=po, c=c, wp=n % 2:
                           e.scalar_tensor_tensor(a_sb[wp][po:po + 64, c, :],
                                                  tcmb[hp][po:po + 64, 256:512],
                                                  lamt_bf[po:po + 64, 0:1],
                                                  tcmb[hp][po:po + 64, 0:256],
                                                  ALU.mult, ALU.add).then_inc(sV.h, 1))
                    comb_done[hh] = sV.inc()

                tail_jobs = []
                tail_epi = {}

                def tail_group(n, sgrp):
                    par = n % 2
                    s, half = [(0, 0), (0, 1), (1, 0), (1, 1)][sgrp]
                    if sgrp == 0:
                        em.wait("tensor", (sV.h, comb_done[n * H + H - 1]))
                        if n == 0:
                            em.wait("tensor", (sK2.h, wb_ld))
                            em.wait("tensor", (sK2S.h, bout_ld))
                    if rel_w[0] is not None:
                        em.wait("tensor", (sV.h, rel_w[0]))
                    em.add("tensor", lambda e, half=half:
                           e.matmul(ps_w[:], ones1[:], bout_sb[:, half * 512:(half + 1) * 512],
                                    start=True, stop=False))
                    for k in range(8):
                        last = k == 7
                        em.add("tensor", lambda e, s=s, half=half, k=k, par=par, last=last:
                               (e.matmul(ps_w[:], a_sb[par][:, k, s * 128:(s + 1) * 128],
                                         wout_sb[:, k, half * 512:(half + 1) * 512],
                                         start=False, stop=last).then_inc(sT.h, 1)
                                if last else
                                e.matmul(ps_w[:], a_sb[par][:, k, s * 128:(s + 1) * 128],
                                         wout_sb[:, k, half * 512:(half + 1) * 512],
                                         start=False, stop=last)))
                    wmm = sT.inc()
                    em.wait("vector", (sT.h, wmm))
                    if sgrp == 0:
                        em.wait("vector", gx_loaded[n][0])
                    if rel_prod[0] is not None:
                        em.wait("vector", (sP.h, rel_prod[0]))
                    em.add("vector", lambda e, s=s, half=half, par=par:
                           e.tensor_tensor(prod[:], ps_w[:],
                                           g_tm[par][:, s, half * 512:(half + 1) * 512],
                                           ALU.mult).then_inc(sV.h, 1))
                    pm = sV.inc()
                    rel_w[0] = pm
                    tail_epi[n][0].append(pm)
                    em.wait("gpsimd", (sV.h, pm))
                    if sgrp == 0:
                        em.wait("gpsimd", gx_loaded[n][1])
                        if n >= 2:
                            em.wait("gpsimd", (sOUT[par].h, 16 * out_cnt[par]))
                    em.add("gpsimd", lambda e, s=s, half=half, par=par:
                           e.tensor_tensor(o_stage[par][:, s, half * 512:(half + 1) * 512], prod[:],
                                           x_win[par][:, s, half * 512:(half + 1) * 512],
                                           ALU.add).then_inc(sP.h, 1))
                    tail_epi[n][1].append(sP.inc())
                    rel_prod[0] = tail_epi[n][1][-1]

                def tail_finish(n):
                    par = n % 2
                    epi_done[n] = (max(tail_epi[n][0]), max(tail_epi[n][1]))
                    em.wait("sync", (sP.h, epi_done[n][1]))
                    em.add("sync", lambda e, n=n, par=par:
                           e.dma_start(out=out[n * 256:(n + 1) * 256, :].rearrange("(a p) e -> p a e", p=128),
                                       in_=o_stage[par][:]).then_inc(sOUT[par].h, 16))
                    out_cnt[par] += 1
                    sOUT[par].inc(16)
                    if n + 2 < NW:
                        prefetch(n + 2)

                def emit_window_tail(n):
                    tail_epi[n] = ([], [])
                    for sgrp in range(4):
                        tail_jobs.append(lambda n=n, sgrp=sgrp: tail_group(n, sgrp))
                    tail_jobs.append(lambda n=n: tail_finish(n))

                for slot in range(NHH + 3):
                    if 2 <= slot <= NHH + 1:
                        emit_V(slot - 2)
                    if slot >= 3:
                        emit_R(slot - 3)
                    if slot < NHH:
                        emit_F(slot)
                    if slot >= 3:
                        nb, hb = divmod(slot - 3, H)
                        if hb == H - 1:
                            emit_window_tail(nb)
                    if tail_jobs:
                        tail_jobs.pop(0)()
                while tail_jobs:
                    tail_jobs.pop(0)()

                em.wait("sync", (sOUT[0].h, sOUT[0].n))
                em.wait("sync", (sOUT[1].h, sOUT[1].n))

                with nc.Block("attn") as blk:
                    em.run(blk)

    return nc


_BUILD_CACHE = {}


def _get_nc():
    if "nc" not in _BUILD_CACHE:
        _BUILD_CACHE["nc"] = build_nc()
    return _BUILD_CACHE["nc"]


def make_in_maps(x, wq1, wq2, wk1, wk2, wv_, wout_, bout_, lvec, wg, wu, wd):
    iden = np.eye(128, dtype=np.float32)
    in_maps = []
    for core in range(NCORE):
        b, qq = core // 4, core % 4
        cos, sin = _core_tables(qq)
        in_maps.append(dict(
            x_attn=np.ascontiguousarray(x[b, 2048 * qq:2048 * qq + AW, :]),
            x_ffn=np.ascontiguousarray(x[b, 4096 * qq:4096 * (qq + 1), :]),
            wq1=wq1, wq2=wq2, wk1=wk1, wk2=wk2, wv=wv_, wout=wout_, bout=bout_,
            lvec=lvec, wg_i=wg, wu_i=wu, wd_i=wd,
            tcos=cos, tsin=sin, paT=_PAT, pbT=_PBT, iden=iden,
        ))
    return in_maps


def kernel(L0_attn_x, L0_attn_Wq1, L0_attn_Wq2, L0_attn_Wk1, L0_attn_Wk2,
           L0_attn_Wv, L0_attn_Wout, L0_attn_bout, L0_attn_lq1, L0_attn_lk1,
           L0_attn_lq2, L0_attn_lk2, L0_ffn_w_gate, L0_ffn_w_up, L0_ffn_w_down):
    x = np.ascontiguousarray(np.asarray(L0_attn_x, dtype=np.float32))
    f32 = lambda a: np.ascontiguousarray(np.asarray(a, dtype=np.float32))
    i32 = lambda a: np.ascontiguousarray(np.asarray(a, dtype=np.int32))
    lvec = np.concatenate([f32(L0_attn_lq1).ravel(), f32(L0_attn_lk1).ravel(),
                           f32(L0_attn_lq2).ravel(), f32(L0_attn_lk2).ravel()]).reshape(1, 256)
    nc = _get_nc()
    in_maps = make_in_maps(x, f32(L0_attn_Wq1), f32(L0_attn_Wq2), f32(L0_attn_Wk1),
                           f32(L0_attn_Wk2), f32(L0_attn_Wv), f32(L0_attn_Wout),
                           f32(L0_attn_bout).reshape(1, E), lvec,
                           i32(L0_ffn_w_gate), i32(L0_ffn_w_up), i32(L0_ffn_w_down))
    res = run_bass_kernel_spmd(nc, in_maps, core_ids=list(range(NCORE)))
    outp = np.empty((B, L, E), dtype=np.float32)
    for core in range(NCORE):
        b, qq = core // 4, core % 4
        outp[b, 4096 * qq:4096 * (qq + 1), :] = res.results[core]["out"]
    return outp
